# version 14
# baseline (speedup 1.0000x reference)
import os
import contextlib
import numpy as np
import ml_dtypes
import concourse.bass as bass
import concourse.mybir as mybir
from concourse.bass_utils import run_bass_kernel_spmd

F32 = mybir.dt.float32
BF16 = mybir.dt.bfloat16
I32 = mybir.dt.int32
AF = mybir.ActivationFunctionType
ALU = mybir.AluOpType
AX = mybir.AxisListType

P = 128
D = 2048
NCH = 16
TOK = 1024
HID = 8192
NHC = 64
EPS = 1e-5
QB = 127.0
N_GV = float(16384 * 2048)
N_WOUT = float(2048 * 8192)
DEBUG = os.environ.get("MK_DEBUG", "0") == "1"


def _region(a):
    ds = mybir.dt.size(a.dtype)
    span = 1
    for stp, cnt in a.ap[1:]:
        span += stp * (cnt - 1)
    lo = a.offset * ds
    return (a.tensor.name, lo, lo + span * ds)


class Sched:

    SY_LANES = 4
    GE_LANES = 2

    def __init__(self):
        self.steps = []
        keys = ["tensor", "vector", "scalar", "gpsimd"]
        keys += [f"sy{j}" for j in range(self.SY_LANES)]
        keys += [f"ge{j}" for j in range(self.GE_LANES)]
        self.keys = keys
        self.eng_n = {e: 0 for e in keys}
        self.lane_rr = {"sync": 0, "gpsimd": 0}
        self.writers = {}
        self.readers = {}

    def _dep_scan(self, reads, writes):
        deps = set()
        for (nm, lo, hi) in reads:
            for (l2, h2, stp) in self.writers.get(nm, ()):
                if l2 < hi and lo < h2:
                    deps.add(stp)
        for (nm, lo, hi) in writes:
            for (l2, h2, stp) in self.writers.get(nm, ()):
                if l2 < hi and lo < h2:
                    deps.add(stp)
            for (l2, h2, stp) in self.readers.get(nm, ()):
                if l2 < hi and lo < h2:
                    deps.add(stp)
        return deps

    def _record(self, reads, writes, stamp):
        for (nm, lo, hi) in writes:
            wl = self.writers.setdefault(nm, [])
            wl[:] = [(l2, h2, stp) for (l2, h2, stp) in wl if not (lo <= l2 and h2 <= hi)]
            wl.append((lo, hi, stamp))
            rl = self.readers.setdefault(nm, [])
            rl[:] = [(l2, h2, stp) for (l2, h2, stp) in rl if not (l2 < hi and lo < h2)]
        for (nm, lo, hi) in reads:
            self.readers.setdefault(nm, []).append((lo, hi, stamp))

    def add(self, eng, kind, fn, r, w):
        reads = [_region(a) for a in r if a is not None]
        writes = [_region(a) for a in w if a is not None]
        deps = self._dep_scan(reads, writes)
        if kind == "c":
            key = eng
        elif eng == "sync":
            key = f"sy{self.lane_rr['sync'] % self.SY_LANES}"
            self.lane_rr["sync"] += 1
        else:
            key = f"ge{self.lane_rr['gpsimd'] % self.GE_LANES}"
            self.lane_rr["gpsimd"] += 1
        idx = self.eng_n[key]
        self.eng_n[key] += 1
        if kind == "d" and idx > 0:
            deps = set(deps)
            deps.add((key, idx - 1))
        stamp = (key, idx)
        self._record(reads, writes, stamp)
        self.steps.append((eng, kind, key, idx, fn, deps))


def build():
    nc = bass.Bass("TRN2", target_bir_lowering=False)

    def din(name, sh, dt):
        return nc.dram_tensor(name, sh, dt, kind="ExternalInput").ap()

    xT = din("xT", [P, NCH * TOK], F32)
    wi = din("wi", [48, P, 2048], F32)
    wo = din("wo", [16, P, 2048], F32)
    wgv = din("wgv", [128, P, 2048], F32)
    wo2 = din("wo2", [4, 64, P, 512], F32)
    ctab = din("ctab", [8, P, TOK], BF16)
    stab = din("stab", [8, P, TOK], BF16)
    vecs = din("vecs", [P, 112], F32)
    id128 = din("id128", [P, P], F32)
    ones16 = din("ones16", [P, P], BF16)
    onesrow = din("onesrow", [1, P], F32)
    onescol = din("onescol", [P, 1], F32)
    perm = din("perm", [P, P], BF16)
    shg = din("shg", [16, P, 2048], F32)
    shw = din("shw", [8, P, 2048], F32)
    out = nc.dram_tensor("out", [TOK, D], F32, kind="ExternalOutput").ap()
    dbg = {}
    if DEBUG:
        for nm, sh, dt in [("dbg_a", [P, 16], F32), ("dbg_r", [P, NCH * TOK], F32),
                           ("dbg_t", [P, NCH * TOK], F32), ("dbg_hq", [P, NCH * TOK], BF16),
                           ("dbg_k1", [P, TOK], F32), ("dbg_mx2", [P, 8], F32),
                           ("dbg_sc", [P, 16], F32)]:
            dbg[nm] = nc.dram_tensor(nm, sh, dt, kind="ExternalOutput").ap()

    ccm_in = nc.dram_tensor("ccm_in", [1, 16], F32, kind="Internal")
    ccm_out = nc.dram_tensor("ccm_out", [1, 16], F32, kind="Internal")
    ccA_in = nc.dram_tensor("ccA_in", [P, 16], F32, kind="Internal")
    ccA_out = nc.dram_tensor("ccA_out", [P, 16], F32, kind="Internal")
    h2T = nc.dram_tensor("h2T", [HID, TOK], F32, kind="Internal")
    h2q = nc.dram_tensor("h2q", [HID, TOK], BF16, kind="Internal")

    st = contextlib.ExitStack()
    sb = lambda name, sh, dt: st.enter_context(nc.sbuf_tensor(name, sh, dt))
    with st:
        Xf = sb("Xf", [P, NCH * TOK], F32)
        arA = sb("arA", [P, NCH * TOK], F32)
        arB = sb("arB", [P, 8192], F32)
        pool = sb("pool", [P, 10 * 1024], F32)
        vec_s = sb("vec_s", [P, 112], F32)
        id_s = sb("id_s", [P, P], F32)
        on16 = sb("on16", [P, P], BF16)
        onr = sb("onr", [1, P], F32)
        onc = sb("onc", [P, 1], F32)
        pm_s = sb("pm_s", [P, P], BF16)
        ccrow = sb("ccrow", [1, 16], F32)
        acol = sb("acol", [P, 16], F32)
        a2col = sb("a2col", [P, 16], F32)
        mcol = sb("mcol", [P, 8], F32)
        mcol2 = sb("mcol2", [P, 8], F32)
        epi2c = sb("epi2c", [P, 8], F32)
        misc = sb("misc", [P, 32], F32)

        p2a = st.enter_context(nc.psum_tensor("p2a", [P, 1024], F32))
        p2b = st.enter_context(nc.psum_tensor("p2b", [P, 1024], F32))
        p2c = st.enter_context(nc.psum_tensor("p2c", [P, 1024], F32))
        p2d = st.enter_context(nc.psum_tensor("p2d", [P, 1024], F32))
        sem_keys = ["tensor", "vector", "scalar", "gpsimd"] + \
            [f"sy{j}" for j in range(Sched.SY_LANES)] + [f"ge{j}" for j in range(Sched.GE_LANES)]
        sem = {k: st.enter_context(nc.semaphore(k)) for k in sem_keys}

        def pt(i):
            return pool[:, i * 1024:(i + 1) * 1024]

        def pt_bf(i):
            return pool.ap().bitcast(BF16)[:, i * 2048:i * 2048 + 1024]

        def pt_bf2(i):
            return pool.ap().bitcast(BF16)[:, i * 2048 + 1024:i * 2048 + 2048]

        def pt_lo(i):
            return pool[:, i * 1024:i * 1024 + 512]

        def pt_hi(i):
            return pool[:, i * 1024 + 512:(i + 1) * 1024]

        def pt_i32(i):
            return pool.ap().bitcast(I32)[:, i * 1024:(i + 1) * 1024]

        arA_bf = arA.ap().bitcast(BF16)
        arB_bf = arB.ap().bitcast(BF16)
        Xf_bf = Xf.ap().bitcast(BF16)

        S = Sched()

        def OP(eng, fn, r, w):
            S.add(eng, "c", fn, r, w)

        def DMA(out_ap, in_ap, via="sync"):
            S.add(via, "d", lambda en, o=out_ap, i=in_ap: en.dma_start(out=o, in_=i),
                  [in_ap], [out_ap])

        TE, VE, SE, GE = "tensor", "vector", "scalar", "gpsimd"

        DMA(vec_s[:], vecs)
        DMA(id_s[:], id128)
        DMA(on16[:], ones16)
        DMA(onr[:], onesrow)
        DMA(onc[:], onescol)
        DMA(pm_s[:], perm)
        OP(VE, lambda e: e.memset(ccrow[:], 0.0), [], [ccrow[:]])
        OP(VE, lambda e: e.memset(misc[:, 28:29], -1.0), [], [misc[:, 28:29]])
        neg1 = misc[:, 28:29]

        for q in range(4):
            DMA(Xf[:, q * 4096:(q + 1) * 4096], xT[:, q * 4096:(q + 1) * 4096])

        wst = [arA[:, 10240:12288], arA[:, 12288:14336]]
        w16v = [arA_bf[:, 28672:28672 + 2048], arA_bf[:, 30720:30720 + 2048]]
        w16r = [arA[:, 14336:15360], arA[:, 15360:16384]]

        def loadw(slot, src):
            DMA(wst[slot], src)
            OP(SE, lambda e, slot=slot: e.activation(out=w16v[slot], in_=wst[slot], func=AF.Identity),
               [wst[slot]], [w16r[slot]])

        x16s = [pt_bf(0), pt_bf(3)]
        x16r = [pt_lo(0), pt_lo(3)]
        xsqs = [pt_bf(1), pt_bf(4)]
        xsqr = [pt_lo(1), pt_lo(4)]
        for ch in range(NCH):
            j = ch % 2
            xch = Xf[:, ch * TOK:(ch + 1) * TOK]
            OP(SE, lambda e, xch=xch, j=j: e.activation(out=x16s[j], in_=xch, func=AF.Identity),
               [xch], [x16r[j]])
            OP(SE, lambda e, j=j: e.activation(out=xsqs[j], in_=x16s[j], func=AF.Square),
               [x16r[j]], [xsqr[j]])
            for hf in range(2):
                OP(TE, lambda e, ch=ch, hf=hf, j=j: e.matmul(
                    p2a[:, hf * 512:(hf + 1) * 512], on16[:], x16s[j][:, hf * 512:(hf + 1) * 512],
                    start=(ch == 0), stop=(ch == NCH - 1)),
                   [on16[:], x16r[j]], [p2a[:, hf * 512:(hf + 1) * 512]])
                OP(TE, lambda e, ch=ch, hf=hf, j=j: e.matmul(
                    p2b[:, hf * 512:(hf + 1) * 512], on16[:], xsqs[j][:, hf * 512:(hf + 1) * 512],
                    start=(ch == 0), stop=(ch == NCH - 1)),
                   [on16[:], xsqr[j]], [p2b[:, hf * 512:(hf + 1) * 512]])
        loadw(0, wi[16, :, :])
        loadw(1, wi[32, :, :])
        OP(VE, lambda e: e.tensor_scalar(out=pt(0), in0=p2a[:], scalar1=1.0 / D, scalar2=None, op0=ALU.mult),
           [p2a[:]], [pt(0)])
        OP(VE, lambda e: e.tensor_scalar(out=pt(1), in0=p2b[:], scalar1=1.0 / D, scalar2=None, op0=ALU.mult),
           [p2b[:]], [pt(1)])
        OP(VE, lambda e: e.tensor_tensor(out=pt(2), in0=pt(0), in1=pt(0), op=ALU.mult), [pt(0)], [pt(2)])
        OP(VE, lambda e: e.tensor_tensor(out=pt(1), in0=pt(1), in1=pt(2), op=ALU.subtract), [pt(1), pt(2)], [pt(1)])
        OP(VE, lambda e: e.tensor_scalar(out=pt(1), in0=pt(1), scalar1=EPS, scalar2=None, op0=ALU.add), [pt(1)], [pt(1)])
        OP(SE, lambda e: e.activation(out=pt(1), in_=pt(1), func=AF.Sqrt), [pt(1)], [pt(1)])
        OP(VE, lambda e: e.reciprocal(out=pt(1), in_=pt(1)), [pt(1)], [pt(1)])
        Xn16 = arB_bf
        for ch in range(NCH):
            xch = Xf[:, ch * TOK:(ch + 1) * TOK]
            xn = arB_bf[:, ch * TOK:(ch + 1) * TOK]
            xnr = arB[:, ch * 512:(ch + 1) * 512]
            OP(VE, lambda e, xch=xch: e.tensor_tensor(out=pt(2), in0=xch, in1=pt(0), op=ALU.subtract),
               [xch, pt(0)], [pt(2)])
            OP(VE, lambda e: e.tensor_tensor(out=pt(2), in0=pt(2), in1=pt(1), op=ALU.mult),
               [pt(2), pt(1)], [pt(2)])
            OP(VE, lambda e, ch=ch, xn=xn, xnr=xnr: e.tensor_scalar(
                out=xn, in0=pt(2), scalar1=vec_s[:, 64 + ch:65 + ch], scalar2=vec_s[:, 80 + ch:81 + ch],
                op0=ALU.mult, op1=ALU.add), [pt(2), vec_s[:]], [xnr])

        ctb = [arA_bf[:, 0:1024], arA_bf[:, 1024:2048]]
        ctb_r = [arA[:, 0:512], arA[:, 512:1024]]
        stb = [arA_bf[:, 18432:19456], arA_bf[:, 19456:20480]]
        stb_r = [arA[:, 9216:9728], arA[:, 9728:10240]]
        z16 = arA_bf[:, 2048:2048 + NCH * TOK]
        z16_r = arA[:, 1024:1024 + 8192]

        def mm_bundle(w16, rhs_bf, slot, nk, k0=0, k1=None):
            if k1 is None:
                k1 = nk
            def fn(e, w16=w16, rhs_bf=rhs_bf, slot=slot, nk=nk, k0=k0, k1=k1):
                last = None
                for k in range(k0, k1):
                    for hf in range(2):
                        last = e.matmul(slot[:, hf * 512:(hf + 1) * 512],
                                        w16[:, k * 128:(k + 1) * 128],
                                        rhs_bf[:, k * TOK + hf * 512: k * TOK + hf * 512 + 512],
                                        start=(k == 0), stop=(k == nk - 1))
                return last
            return fn

        def bundle_split(w16, wreg, rhs_bf, slot):
            OP(TE, mm_bundle(w16, rhs_bf, slot, NCH, 0, 8), [wreg, arB[:, 0:4096]], [slot[:]])
            OP(TE, mm_bundle(w16, rhs_bf, slot, NCH, 8, 16), [wreg, arB[:, 4096:8192]], [slot[:]])

        def bundle_quarters(w16, wreg, rhs_bf, slot):
            for q in range(4):
                OP(TE, mm_bundle(w16, rhs_bf, slot, NCH, q * 4, (q + 1) * 4),
                   [wreg, arB[:, q * 2048:(q + 1) * 2048]], [slot[:]])

        def permmm(src16, srcr, slot):
            def emit():
                OP(TE, lambda e, src16=src16, slot=slot: [
                    e.matmul(slot[:, 0:512], pm_s[:], src16[:, 0:512], start=True, stop=True),
                    e.matmul(slot[:, 512:1024], pm_s[:], src16[:, 512:1024], start=True, stop=True)][-1],
                   [pm_s[:], srcr], [slot[:]])
            return emit

        k16s = [pt_bf(0), pt_bf(2)]
        k16r = [pt_lo(0), pt_lo(2)]
        sqKs = [pt_bf2(0), pt_bf2(2)]
        sqKr = [pt_hi(0), pt_hi(2)]
        v16s = [pt_bf(1), pt_bf(3)]
        v16r = [pt_lo(1), pt_lo(3)]
        sqVs = [pt_bf2(1), pt_bf2(3)]
        sqVr = [pt_hi(1), pt_hi(3)]
        rotKs = [pt_bf(4), pt_bf(5)]
        rotKr = [pt_lo(4), pt_lo(5)]
        rotVs = [pt_bf2(4), pt_bf2(5)]
        rotVr = [pt_hi(4), pt_hi(5)]

        def stats_mm(ch, src16, srcr, slot):
            for hf in range(2):
                OP(TE, lambda e, ch=ch, hf=hf, src16=src16, slot=slot: e.matmul(
                    slot[:, hf * 512:(hf + 1) * 512], on16[:], src16[:, hf * 512:(hf + 1) * 512],
                    start=(ch == 0), stop=(ch == NCH - 1)),
                   [on16[:], srcr], [slot[:, hf * 512:(hf + 1) * 512]])

        DMA(ctb[0], ctab[0, :, :])
        DMA(stb[0], stab[0, :, :])
        for ch in range(NCH):
            j = ch % 2
            mK, mV = 16 + ch, 32 + ch
            bundle_split(w16v[0], w16r[0], Xn16, p2a)
            OP(SE, lambda e, mK=mK, j=j: e.activation(
                out=k16s[j], in_=p2a[:], func=AF.Identity, bias=vec_s[:, mK:mK + 1]),
               [p2a[:], vec_s[:]], [k16r[j]])
            if ch < 8:
                permmm(k16s[j], k16r[j], p2a)()
                OP(SE, lambda e, j=j: e.activation(out=rotKs[j], in_=p2a[:], func=AF.Identity),
                   [p2a[:]], [rotKr[j]])
            if ch < NCH - 1:
                loadw(0, wi[16 + ch + 1, :, :])
            bundle_split(w16v[1], w16r[1], Xn16, p2b)
            if ch > 0:
                jp = (ch - 1) % 2
                stats_mm(ch - 1, sqKs[jp], sqKr[jp], p2c)
                stats_mm(ch - 1, sqVs[jp], sqVr[jp], p2d)
            OP(SE, lambda e, mV=mV, j=j: e.activation(
                out=v16s[j], in_=p2b[:], func=AF.Identity, bias=vec_s[:, mV:mV + 1]),
               [p2b[:], vec_s[:]], [v16r[j]])
            if ch < 8:
                permmm(v16s[j], v16r[j], p2b)()
                OP(SE, lambda e, j=j: e.activation(out=rotVs[j], in_=p2b[:], func=AF.Identity),
                   [p2b[:]], [rotVr[j]])
            if ch < NCH - 1:
                loadw(1, wi[32 + ch + 1, :, :])
            OP(SE, lambda e, j=j: e.activation(out=sqKs[j], in_=k16s[j], func=AF.Square),
               [k16r[j]], [sqKr[j]])
            OP(SE, lambda e, j=j: e.activation(out=sqVs[j], in_=v16s[j], func=AF.Square),
               [v16r[j]], [sqVr[j]])
            zch = z16[:, ch * TOK:(ch + 1) * TOK]
            zchr = z16_r[:, ch * 512:(ch + 1) * 512]
            if ch < 8:
                ti = ch % 2
                if ch < 7:
                    DMA(ctb[(ch + 1) % 2], ctab[ch + 1, :, :])
                    DMA(stb[(ch + 1) % 2], stab[ch + 1, :, :])
                OP(VE, lambda e, j=j, ti=ti: e.tensor_tensor(out=pt_bf(6), in0=k16s[j], in1=ctb[ti], op=ALU.mult),
                   [k16r[j], ctb_r[ti]], [pt_lo(6)])
                OP(VE, lambda e, j=j, ti=ti: e.tensor_tensor(out=pt_bf2(6), in0=rotKs[j], in1=stb[ti], op=ALU.mult),
                   [rotKr[j], stb_r[ti]], [pt_hi(6)])
                OP(VE, lambda e: e.tensor_tensor(out=pt_bf(7), in0=pt_bf(6), in1=pt_bf2(6), op=ALU.add),
                   [pt_lo(6), pt_hi(6)], [pt_lo(7)])
                OP(VE, lambda e, j=j, ti=ti: e.tensor_tensor(out=pt_bf(6), in0=v16s[j], in1=ctb[ti], op=ALU.mult),
                   [v16r[j], ctb_r[ti]], [pt_lo(6)])
                OP(VE, lambda e, j=j, ti=ti: e.tensor_tensor(out=pt_bf2(6), in0=rotVs[j], in1=stb[ti], op=ALU.mult),
                   [rotVr[j], stb_r[ti]], [pt_hi(6)])
                OP(VE, lambda e: e.tensor_tensor(out=pt_bf2(7), in0=pt_bf(6), in1=pt_bf2(6), op=ALU.add),
                   [pt_lo(6), pt_hi(6)], [pt_hi(7)])
                OP(VE, lambda e, zch=zch, zchr=zchr: e.tensor_tensor(
                    out=zch, in0=pt_bf(7), in1=pt_bf2(7), op=ALU.mult),
                   [pt_lo(7), pt_hi(7)], [zchr])
            else:
                OP(VE, lambda e, zch=zch, zchr=zchr, j=j: e.tensor_tensor(
                    out=zch, in0=k16s[j], in1=v16s[j], op=ALU.mult), [k16r[j], v16r[j]], [zchr])
        stats_mm(NCH - 1, sqKs[(NCH - 1) % 2], sqKr[(NCH - 1) % 2], p2c)
        stats_mm(NCH - 1, sqVs[(NCH - 1) % 2], sqVr[(NCH - 1) % 2], p2d)

        OP(VE, lambda e: e.tensor_copy(out=pt(7), in_=p2c[:]), [p2c[:]], [pt(7)])
        OP(VE, lambda e: e.tensor_tensor(out=pt(7), in0=pt(7), in1=p2d[:], op=ALU.mult), [pt(7), p2d[:]], [pt(7)])
        OP(SE, lambda e: e.activation(out=pt(7), in_=pt(7), func=AF.Sqrt), [pt(7)], [pt(7)])
        OP(VE, lambda e: e.reciprocal(out=pt(7), in_=pt(7)), [pt(7)], [pt(7)])
        for ch in range(NCH):
            zch = z16[:, ch * TOK:(ch + 1) * TOK]
            zchr = z16_r[:, ch * 512:(ch + 1) * 512]
            OP(VE, lambda e, zch=zch: e.tensor_tensor(out=pt(2), in0=zch, in1=pt(7), op=ALU.mult),
               [zchr, pt(7)], [pt(2)])
            OP(VE, lambda e, ch=ch: e.tensor_reduce(
                out=acol[:, ch:ch + 1], in_=pt(2), axis=AX.X, op=ALU.add), [pt(2)], [acol[:, ch:ch + 1]])
        DMA(ccA_in.ap(), acol[:], via="gpsimd")
        OP(GE, lambda e: e.collective_compute(
            "AllReduce", ALU.add, replica_groups=[[0, 1], [2, 3], [4, 5], [6, 7]],
            ins=[ccA_in.ap().opt()], outs=[ccA_out.ap().opt()]),
           [ccA_in.ap()], [ccA_out.ap()])
        DMA(a2col[:], ccA_out.ap(), via="gpsimd")
        OP(VE, lambda e: e.tensor_scalar(out=a2col[:], in0=a2col[:],
                                         scalar1=0.5, scalar2=0.5, op0=ALU.mult, op1=ALU.add),
           [a2col[:]], [a2col[:]])
        if DEBUG:
            DMA(dbg["dbg_a"], a2col[:], via="gpsimd")

        AQ16 = arA_bf[:, 0:NCH * TOK]
        AQ16_r = arA[:, 0:8192]
        loadw(0, wi[0, :, :])
        for ch in range(NCH):
            j = ch % 2
            slot = p2a if j == 0 else p2b
            aq = AQ16[:, ch * TOK:(ch + 1) * TOK]
            aqr = arA[:, ch * 512:(ch + 1) * 512]
            OP(TE, mm_bundle(w16v[j], Xn16, slot, NCH), [w16r[j], arB[:]], [slot[:]])
            if ch < NCH - 1:
                loadw((ch + 1) % 2, wi[ch + 1, :, :])
            OP(SE, lambda e, ch=ch, aq=aq, slot=slot: e.activation(
                out=aq, in_=slot[:], func=AF.Identity, bias=vec_s[:, ch:ch + 1]),
               [slot[:], vec_s[:]], [aqr])
            OP(VE, lambda e, ch=ch, aq=aq, aqr=aqr: e.tensor_scalar(
                out=aq, in0=aq, scalar1=a2col[:, ch:ch + 1], scalar2=None, op0=ALU.mult),
               [aqr, a2col[:]], [aqr])

        shstg = [arB[:, 0:2048], arB[:, 2048:4096]]

        def shard_unit(i):
            src = shg[i, :, :] if i < 16 else shw[i - 16, :, :]
            stg = shstg[i % 2]
            red = misc[:, 8:10] if i % 2 == 0 else misc[:, 24:26]
            red1 = misc[:, 10:11] if i % 2 == 0 else misc[:, 26:27]
            DMA(stg, src, via="gpsimd")
            OP(VE, lambda e, stg=stg, red=red: e.tensor_reduce(
                out=red, in_=stg.rearrange("p (a b) -> p a b", a=2),
                axis=AX.X, op=ALU.add, apply_absolute_value=True), [stg], [red])
            OP(VE, lambda e, red=red, red1=red1: e.tensor_reduce(
                out=red1, in_=red, axis=AX.X, op=ALU.add), [red], [red1])
            if i == 0:
                OP(VE, lambda e, red1=red1: e.tensor_copy(out=misc[:, 11:12], in_=red1),
                   [red1], [misc[:, 11:12]])
            elif i == 16:
                OP(VE, lambda e, red1=red1: e.tensor_copy(out=misc[:, 12:13], in_=red1),
                   [red1], [misc[:, 12:13]])
            else:
                dst = misc[:, 11:12] if i < 16 else misc[:, 12:13]
                OP(VE, lambda e, dst=dst, red1=red1: e.tensor_tensor(
                    out=dst, in0=dst, in1=red1, op=ALU.add), [dst, red1], [dst])

        loadw(0, wo[0, :, :])
        for m in range(16):
            j = m % 2
            slot = p2a if j == 0 else p2b
            xch = Xf[:, m * TOK:(m + 1) * TOK]
            OP(TE, mm_bundle(w16v[j], AQ16, slot, NCH), [w16r[j], AQ16_r], [slot[:]])
            if m < 15:
                loadw((m + 1) % 2, wo[m + 1, :, :])
            OP(VE, lambda e, xch=xch, slot=slot: e.scalar_tensor_tensor(
                out=pt(2), in0=xch, scalar=2.0, in1=slot[:], op0=ALU.mult, op1=ALU.add),
               [xch, slot[:]], [pt(2)])
            OP(VE, lambda e, m=m, xch=xch: e.tensor_scalar(
                out=xch, in0=pt(2), scalar1=vec_s[:, 48 + m:49 + m], scalar2=None, op0=ALU.add),
               [pt(2), vec_s[:]], [xch])
            OP(SE, lambda e, xch=xch, j=j: e.activation(out=x16s[j], in_=xch, func=AF.Identity),
               [xch], [x16r[j]])
            OP(SE, lambda e, j=j: e.activation(out=xsqs[j], in_=x16s[j], func=AF.Square),
               [x16r[j]], [xsqr[j]])
            if m > 0:
                jp = (m - 1) % 2
                stats_mm(m - 1, x16s[jp], x16r[jp], p2c)
                stats_mm(m - 1, xsqs[jp], xsqr[jp], p2d)
            if m < 12:
                shard_unit(2 * m)
                shard_unit(2 * m + 1)
        stats_mm(15, x16s[15 % 2], x16r[15 % 2], p2c)
        stats_mm(15, xsqs[15 % 2], xsqr[15 % 2], p2d)

        OP(TE, lambda e: e.matmul(p2a[0:1, 0:1], onc[:], misc[:, 11:12], start=True, stop=True),
           [onc[:], misc[:, 11:12]], [p2a[0:1, 0:1]])
        OP(VE, lambda e: e.tensor_copy(out=ccrow[0:1, 0:1], in_=p2a[0:1, 0:1]),
           [p2a[0:1, 0:1]], [ccrow[0:1, 0:1]])
        OP(TE, lambda e: e.matmul(p2a[0:1, 1:2], onc[:], misc[:, 12:13], start=True, stop=True),
           [onc[:], misc[:, 12:13]], [p2a[0:1, 1:2]])
        OP(VE, lambda e: e.tensor_copy(out=ccrow[0:1, 1:2], in_=p2a[0:1, 1:2]),
           [p2a[0:1, 1:2]], [ccrow[0:1, 1:2]])
        DMA(ccm_in.ap(), ccrow[:], via="gpsimd")
        OP(GE, lambda e: e.collective_compute(
            "AllReduce", ALU.add, replica_groups=[list(range(8))],
            ins=[ccm_in.ap().opt()], outs=[ccm_out.ap().opt()]),
           [ccm_in.ap()], [ccm_out.ap()])
        DMA(ccrow[:], ccm_out.ap(), via="gpsimd")

        def emit_cc1_tail():
            OP(TE, lambda e: e.matmul(p2a[:, 0:16], onr[:], ccrow[:], start=True, stop=True),
               [onr[:], ccrow[:]], [p2a[:, 0:16]])
            OP(VE, lambda e: e.tensor_copy(out=a2col[:, 0:16], in_=p2a[:, 0:16]),
               [p2a[:, 0:16]], [a2col[:]])
            for (c0, c1, nrm) in ((0, 16, N_GV), (1, 17, N_WOUT)):
                OP(VE, lambda e, c0=c0, c1=c1, nrm=nrm: e.tensor_scalar(
                    out=misc[:, c1:c1 + 1], in0=a2col[:, c0:c0 + 1],
                    scalar1=1.0 / nrm, scalar2=1e-5, op0=ALU.mult, op1=ALU.max),
                   [a2col[:]], [misc[:, c1:c1 + 1]])
            OP(VE, lambda e: e.reciprocal(out=misc[:, 18:19], in_=misc[:, 16:17]), [misc[:, 16:17]], [misc[:, 18:19]])
            OP(VE, lambda e: e.reciprocal(out=misc[:, 19:20], in_=misc[:, 17:18]), [misc[:, 17:18]], [misc[:, 19:20]])
            for (dsti, srci, k) in ((20, 16, 0.5), (21, 16, -0.5), (22, 17, 0.5), (23, 17, -0.5)):
                OP(VE, lambda e, dsti=dsti, srci=srci, k=k: e.tensor_scalar(
                    out=misc[:, dsti:dsti + 1], in0=misc[:, srci:srci + 1], scalar1=k, scalar2=None, op0=ALU.mult),
                   [misc[:, srci:srci + 1]], [misc[:, dsti:dsti + 1]])
            if DEBUG:
                OP(VE, lambda e: e.tensor_copy(out=a2col[:, 2:3], in_=misc[:, 16:17]), [misc[:, 16:17]], [a2col[:, 2:3]])
                OP(VE, lambda e: e.tensor_copy(out=a2col[:, 3:4], in_=misc[:, 17:18]), [misc[:, 17:18]], [a2col[:, 3:4]])
                DMA(dbg["dbg_sc"], a2col[:, 0:16], via="gpsimd")

        OP(VE, lambda e: e.tensor_scalar(out=pt(0), in0=p2c[:], scalar1=1.0 / D, scalar2=None, op0=ALU.mult),
           [p2c[:]], [pt(0)])
        OP(VE, lambda e: e.tensor_scalar(out=pt(1), in0=p2d[:], scalar1=1.0 / D, scalar2=None, op0=ALU.mult),
           [p2d[:]], [pt(1)])
        OP(VE, lambda e: e.tensor_tensor(out=pt(2), in0=pt(0), in1=pt(0), op=ALU.mult), [pt(0)], [pt(2)])
        OP(VE, lambda e: e.tensor_tensor(out=pt(1), in0=pt(1), in1=pt(2), op=ALU.subtract), [pt(1), pt(2)], [pt(1)])
        OP(VE, lambda e: e.tensor_scalar(out=pt(1), in0=pt(1), scalar1=EPS, scalar2=None, op0=ALU.add), [pt(1)], [pt(1)])
        OP(SE, lambda e: e.activation(out=pt(1), in_=pt(1), func=AF.Sqrt), [pt(1)], [pt(1)])
        OP(VE, lambda e: e.reciprocal(out=pt(1), in_=pt(1)), [pt(1)], [pt(1)])
        for ch in range(NCH):
            xch = Xf[:, ch * TOK:(ch + 1) * TOK]
            OP(VE, lambda e, xch=xch: e.tensor_tensor(out=pt(2), in0=xch, in1=pt(0), op=ALU.subtract),
               [xch, pt(0)], [pt(2)])
            OP(VE, lambda e, xch=xch: e.tensor_tensor(out=xch, in0=pt(2), in1=pt(1), op=ALU.mult),
               [pt(2), pt(1)], [xch])
        if DEBUG:
            DMA(dbg["dbg_r"], Xf[:], via="gpsimd")

        for ch in range(NCH):
            j = ch % 2
            xch = Xf[:, ch * TOK:(ch + 1) * TOK]
            tch = arA[:, ch * TOK:(ch + 1) * TOK]
            OP(SE, lambda e, xch=xch, j=j: e.activation(out=x16s[j], in_=xch, func=AF.Square),
               [xch], [x16r[j]])
            if ch > 0:
                jp = (ch - 1) % 2
                stats_mm(ch - 1, x16s[jp], x16r[jp], p2c)
            OP(VE, lambda e, xch=xch, tch=tch, ch=ch: e.tensor_scalar(
                out=tch, in0=xch, scalar1=vec_s[:, 96 + ch:97 + ch], scalar2=None, op0=ALU.mult),
               [xch, vec_s[:]], [tch])
        stats_mm(15, x16s[15 % 2], x16r[15 % 2], p2c)
        tt = arA
        if DEBUG:
            DMA(dbg["dbg_t"], arA[:], via="gpsimd")
        OP(VE, lambda e: e.tensor_scalar(out=pt(0), in0=p2c[:], scalar1=1.0 / D, scalar2=EPS, op0=ALU.mult, op1=ALU.add),
           [p2c[:]], [pt(0)])
        OP(SE, lambda e: e.activation(out=pt(0), in_=pt(0), func=AF.Sqrt), [pt(0)], [pt(0)])
        OP(VE, lambda e: e.reciprocal(out=pt(0), in_=pt(0)), [pt(0)], [pt(0)])
        OP(SE, lambda e: e.activation(out=pt(1), in_=tt[:, 0:TOK], func=AF.Abs), [tt[:, 0:TOK]], [pt(1)])
        abst = [pt(2), pt(5)]
        for ch in range(1, NCH):
            j = ch % 2
            tch = tt[:, ch * TOK:(ch + 1) * TOK]
            OP(SE, lambda e, tch=tch, j=j: e.activation(out=abst[j], in_=tch, func=AF.Abs), [tch], [abst[j]])
            OP(VE, lambda e, j=j: e.tensor_tensor(out=pt(1), in0=pt(1), in1=abst[j], op=ALU.max),
               [pt(1), abst[j]], [pt(1)])
        for j in range(8):
            OP(TE, lambda e, j=j: e.transpose(p2d[:, 0:128], pt(1)[:, j * 128:(j + 1) * 128], id_s[:]),
               [pt(1), id_s[:]], [p2d[:, 0:128]])
            OP(VE, lambda e, j=j: e.tensor_reduce(
                out=mcol[:, j:j + 1], in_=p2d[:, 0:128], axis=AX.X, op=ALU.max),
               [p2d[:, 0:128]], [mcol[:, j:j + 1]])
        emit_cc1_tail()
        for j in range(8):
            OP(TE, lambda e, j=j: e.transpose(p2d[0:1, j * 128:(j + 1) * 128], mcol[:, j:j + 1], id_s[:]),
               [mcol[:], id_s[:]], [p2d[0:1, j * 128:(j + 1) * 128]])
        OP(VE, lambda e: e.tensor_copy(out=pt(2)[0:1, :], in_=p2d[0:1, :]), [p2d[:]], [pt(2)])
        OP(VE, lambda e: e.tensor_tensor(out=pt(2)[0:1, :], in0=pt(2)[0:1, :], in1=pt(0)[0:1, :], op=ALU.mult),
           [pt(2), pt(0)], [pt(2)])
        OP(VE, lambda e: e.tensor_scalar(out=pt(2)[0:1, :], in0=pt(2)[0:1, :], scalar1=1e-5, scalar2=None, op0=ALU.max),
           [pt(2)], [pt(2)])
        for hf in range(2):
            OP(TE, lambda e, hf=hf: e.matmul(
                p2c[:, hf * 512:(hf + 1) * 512], onr[:], pt(2)[0:1, hf * 512:(hf + 1) * 512],
                start=True, stop=True), [onr[:], pt(2)], [p2c[:, hf * 512:(hf + 1) * 512]])
        OP(VE, lambda e: e.reciprocal(out=pt(3), in_=p2c[:]), [p2c[:]], [pt(3)])
        OP(VE, lambda e: e.scalar_tensor_tensor(
            out=pt(4), in0=pt(0), scalar=QB, in1=pt(3), op0=ALU.mult, op1=ALU.mult),
           [pt(0), pt(3)], [pt(4)])
        OP(VE, lambda e: e.tensor_scalar(
            out=pt(5), in0=p2c[:], scalar1=misc[:, 16:17], scalar2=1.0 / QB, op0=ALU.mult, op1=ALU.mult),
           [p2c[:], misc[:, 16:17]], [pt(5)])
        if DEBUG:
            DMA(dbg["dbg_k1"], pt(4), via="gpsimd")
        Vst = [arA[:, 0:2048], arA[:, 2048:4096]]
        Gst = [arA[:, 4096:6144], arA[:, 6144:8192]]
        tmpVA, tmpVAr = arA_bf[:, 16384:18432], arA[:, 8192:9216]
        tmpVB, tmpVBr = arA_bf[:, 18432:20480], arA[:, 9216:10240]
        tmpGA, tmpGAr = arA_bf[:, 20480:22528], arA[:, 10240:11264]
        tmpGB, tmpGBr = arA_bf[:, 22528:24576], arA[:, 11264:12288]
        TV = [arA_bf[:, 24576:26624], arA_bf[:, 26624:28672]]
        TVr = [arA[:, 12288:13312], arA[:, 13312:14336]]
        TG = [arA_bf[:, 28672:30720], arA_bf[:, 30720:32768]]
        TGr = [arA[:, 14336:15360], arA[:, 15360:16384]]

        def tern_quant(src, mk, tn, tnr, hi_col, nhi_col):
            OP(VE, lambda e, src=src, mk=mk: e.tensor_scalar(
                out=mk, in0=src, scalar1=misc[:, hi_col:hi_col + 1], scalar2=None, op0=ALU.is_gt),
               [src, misc[:, hi_col:hi_col + 1]], [mk])
            OP(VE, lambda e, src=src, mk=mk, tn=tn, tnr=tnr: e.scalar_tensor_tensor(
                out=tn, in0=src, scalar=misc[:, nhi_col:nhi_col + 1], in1=mk, op0=ALU.is_lt, op1=ALU.subtract),
               [src, mk, misc[:, nhi_col:nhi_col + 1]], [tnr])

        Mv = arA[:, 8192:10240]
        Mg = arA[:, 10240:12288]

        def q_val(a):
            i = a % 2
            DMA(Vst[i], wgv[a, :, :])
            tern_quant(Vst[i], Mv, TV[i], TVr[i], 20, 21)

        def q_gate(a):
            i = a % 2
            DMA(Gst[i], wgv[64 + a, :, :])
            tern_quant(Gst[i], Mg, TG[i], TGr[i], 20, 21)


        hq = arB_bf
        hv = [pt(6), pt(8)]
        hi32 = [7, 9]
        hbk = [pt(2), pt(0)]
        hs1 = [pt(3), pt(1)]

        def hq_mult(ch):
            j = ch % 2
            tch = tt[:, ch * TOK:(ch + 1) * TOK]
            OP(VE, lambda e, tch=tch, j=j: e.tensor_tensor(out=hv[j], in0=tch, in1=pt(4), op=ALU.mult),
               [tch, pt(4)], [hv[j]])

        def hq_casts(ch):
            j = ch % 2
            OP(SE, lambda e, j=j: e.activation(out=pt_i32(hi32[j]), in_=hv[j], func=AF.Identity),
               [hv[j]], [pt(hi32[j])])
            OP(SE, lambda e, j=j: e.activation(out=hbk[j], in_=pt_i32(hi32[j]), func=AF.Identity),
               [pt(hi32[j])], [hbk[j]])

        def hq_corr(ch):
            j = ch % 2
            hqc = arB_bf[:, ch * TOK:(ch + 1) * TOK]
            hqr = arB[:, ch * 512:(ch + 1) * 512]
            OP(VE, lambda e, j=j: e.tensor_tensor(out=hv[j], in0=hv[j], in1=hbk[j], op=ALU.subtract),
               [hv[j], hbk[j]], [hv[j]])
            OP(VE, lambda e, j=j: e.scalar_tensor_tensor(
                out=hs1[j], in0=hv[j], scalar=0.5, in1=hbk[j], op0=ALU.is_gt, op1=ALU.add),
               [hv[j], hbk[j]], [hs1[j]])
            OP(VE, lambda e, j=j, hqc=hqc, hqr=hqr: e.scalar_tensor_tensor(
                out=hqc, in0=hv[j], scalar=-0.5, in1=hs1[j], op0=ALU.is_lt, op1=ALU.subtract),
               [hv[j], hs1[j]], [hqr])

        hq_mult(0)
        for ch in range(NCH):
            hq_casts(ch)
            if ch < NCH - 1:
                hq_mult(ch + 1)
            if ch == 11:
                q_val(0)
            if ch == 12:
                q_val(1)
            if ch == 14:
                q_gate(0)
            if ch == 15:
                q_gate(1)
            hq_corr(ch)
        if DEBUG:
            DMA(dbg["dbg_hq"], arB_bf[:, 0:NCH * TOK], via="gpsimd")

        vtile = [pt(1), pt(2)]
        gtile = [pt(3), pt(8)]
        htile = [pt(9), pt(7)]
        mx2 = pt(6)

        def pv_of(a):
            return p2a if a % 2 == 0 else p2c

        def emit_ht(a):
            i = a % 2
            gt, ht = gtile[i], htile[i]
            pv = pv_of(a)
            OP(VE, lambda e, gt=gt, ht=ht, pv=pv: e.tensor_tensor(out=ht, in0=gt, in1=pv[:], op=ALU.mult),
               [gt, pv[:]], [ht])
            if a == 0:
                OP(SE, lambda e, ht=ht: e.activation(out=mx2, in_=ht, func=AF.Abs), [ht], [mx2])
            else:
                OP(SE, lambda e, ht=ht, i=i: e.activation(out=vtile[i], in_=ht, func=AF.Abs),
                   [ht], [vtile[i]])
            DMA(h2T.ap()[a * 128:(a + 1) * 128, :], ht, via="gpsimd")

        def emit_max(a):
            if a == 0:
                return
            i = a % 2
            OP(VE, lambda e, i=i: e.tensor_tensor(out=mx2, in0=mx2, in1=vtile[i], op=ALU.max),
               [mx2, vtile[i]], [mx2])

        for a in range(NHC):
            i = a % 2
            pv = pv_of(a)
            pg = p2b if i == 0 else p2d
            gt = gtile[i]
            bsel = bundle_quarters if a <= 1 else bundle_split
            bsel(TV[i], TVr[i], hq, pv)
            bsel(TG[i], TGr[i], hq, pg)
            if 2 <= a + 1 < NHC:
                q_val(a + 1)
                q_gate(a + 1)
            if a >= 1:
                emit_ht(a - 1)
            if a >= 2:
                emit_max(a - 2)
            OP(VE, lambda e, gt=gt, pg=pg: e.tensor_tensor(out=gt, in0=pg[:], in1=pt(5), op=ALU.mult),
               [pg[:], pt(5)], [gt])
            OP(SE, lambda e, gt=gt: e.activation(out=gt, in_=gt, func=AF.Silu), [gt], [gt])
        emit_ht(NHC - 1)
        emit_max(NHC - 2)
        emit_max(NHC - 1)

        OP(VE, lambda e: e.tensor_tensor(out=mx2, in0=mx2, in1=pt(5), op=ALU.mult),
           [mx2, pt(5)], [mx2])
        for j in range(8):
            OP(TE, lambda e, j=j: e.transpose(p2d[:, 0:128], mx2[:, j * 128:(j + 1) * 128], id_s[:]),
               [mx2, id_s[:]], [p2d[:, 0:128]])
            OP(VE, lambda e, j=j: e.tensor_reduce(
                out=mcol2[:, j:j + 1], in_=p2d[:, 0:128], axis=AX.X, op=ALU.max),
               [p2d[:, 0:128]], [mcol2[:, j:j + 1]])
        OP(VE, lambda e: e.tensor_scalar(out=mcol2[:], in0=mcol2[:], scalar1=1e-5, scalar2=None, op0=ALU.max),
           [mcol2[:]], [mcol2[:]])
        if DEBUG:
            DMA(dbg["dbg_mx2"], mcol2[:], via="gpsimd")
        OP(VE, lambda e: e.tensor_scalar(
            out=epi2c[:], in0=mcol2[:], scalar1=misc[:, 17:18], scalar2=1.0 / QB, op0=ALU.mult, op1=ALU.mult),
           [mcol2[:], misc[:, 17:18]], [epi2c[:]])
        OP(VE, lambda e: e.reciprocal(out=mcol2[:], in_=mcol2[:]), [mcol2[:]], [mcol2[:]])
        OP(VE, lambda e: e.tensor_scalar(out=mcol2[:], in0=mcol2[:], scalar1=QB, scalar2=None, op0=ALU.mult),
           [mcol2[:]], [mcol2[:]])
        for j in range(8):
            OP(TE, lambda e, j=j: e.transpose(p2d[0:1, j * 128:(j + 1) * 128], mcol2[:, j:j + 1], id_s[:]),
               [mcol2[:], id_s[:]], [p2d[0:1, j * 128:(j + 1) * 128]])
        OP(VE, lambda e: e.tensor_copy(out=pt(2)[0:1, :], in_=p2d[0:1, :]), [p2d[:]], [pt(2)])
        for hf in range(2):
            OP(TE, lambda e, hf=hf: e.matmul(
                p2c[:, hf * 512:(hf + 1) * 512], onr[:], pt(2)[0:1, hf * 512:(hf + 1) * 512],
                start=True, stop=True), [onr[:], pt(2)], [p2c[:, hf * 512:(hf + 1) * 512]])
        OP(VE, lambda e: e.tensor_copy(out=pt(0), in_=p2c[:]), [p2c[:]], [pt(0)])
        OP(VE, lambda e: e.tensor_tensor(out=pt(0), in0=pt(0), in1=pt(5), op=ALU.mult),
           [pt(0), pt(5)], [pt(0)])

        rT = arA
        for tc in range(8):
            pd = p2d if tc % 2 == 0 else p2c
            for half in range(2):
                for fo in range(8):
                    fc = half * 8 + fo
                    src = Xf[:, fc * TOK + tc * 128: fc * TOK + tc * 128 + 128]
                    OP(TE, lambda e, src=src, fo=fo, pd=pd: e.transpose(
                        pd[:, fo * 128:(fo + 1) * 128], src, id_s[:]),
                       [src, id_s[:]], [pd[:, fo * 128:(fo + 1) * 128]])
                dstv = arA[:, tc * 2048 + half * 1024: tc * 2048 + half * 1024 + 1024]
                OP(SE, lambda e, dstv=dstv, pd=pd: e.activation(out=dstv, in_=pd[:], func=AF.Identity),
                   [pd[:]], [dstv])

        ternW = Xf_bf[:, 0:NHC * 512]
        ternWr = Xf[:, 0:NHC * 256]
        w7st = [arB[:, j * 512:(j + 1) * 512] for j in range(4)]
        w7mks = [arB[:, 2048:2560], arB[:, 4608:5120]]
        h2qb = [arB_bf[:, 5120 + j * TOK:5120 + (j + 1) * TOK] for j in range(4)]
        h2qbr = [arB[:, 2560 + j * 512:2560 + (j + 1) * 512] for j in range(4)]
        sqbuf = Xf[:, 0:2048]
        outbufs = [Xf[:, 2048:4096], Xf[:, 4096:6144]]
        slots = [p2a[:, 0:512], p2a[:, 512:1024], p2b[:, 0:512], p2b[:, 512:1024],
                 p2c[:, 0:512], p2c[:, 512:1024], p2d[:, 0:512], p2d[:, 512:1024]]
        psum_all = [p2a[:], p2b[:], p2c[:], p2d[:]]
        rbt = [pt(1), pt(2), pt(4)]
        tfq = [pt(6), pt(7)]
        qv = [pt_bf(9), pt_bf(5)]
        qvr = [pt_lo(9), pt_lo(5)]

        def quantize_w7(g, k):
            j = k % 4
            i = k % 2
            DMA(w7st[j], wo2[g, k, :, :])
            twk = ternW[:, k * 512:(k + 1) * 512]
            twkr = ternWr[:, k * 256:(k + 1) * 256]
            tern_quant(w7st[j], w7mks[i], twk, twkr, 22, 23)

        def f2_bundle(g, k, lhs_bf, lhs_r):
            if k == 0:
                for m in range(8):
                    OP(TE, lambda e, k=k, lhs_bf=lhs_bf, m=m: e.matmul(
                        slots[m], lhs_bf[:, m * 128:(m + 1) * 128],
                        ternW[:, k * 512:(k + 1) * 512], start=True, stop=False),
                       [lhs_r, ternWr[:, k * 256:(k + 1) * 256]], [slots[m]])
                return
            def fn(e, k=k, lhs_bf=lhs_bf):
                last = None
                for m in range(8):
                    last = e.matmul(slots[m], lhs_bf[:, m * 128:(m + 1) * 128],
                                    ternW[:, k * 512:(k + 1) * 512],
                                    start=(k == 0), stop=(k == NHC - 1))
                return last
            OP(TE, fn, [lhs_r, ternWr[:, k * 256:(k + 1) * 256]], psum_all)

        for kk in range(3):
            DMA(rbt[kk], h2T.ap()[kk * 128:(kk + 1) * 128, :], via="gpsimd")
        quantize_w7(0, 0)
        quantize_w7(0, 1)

        def g0_stageA(k):
            i = k % 2
            rb, tq = rbt[k % 3], tfq[i]
            OP(VE, lambda e, rb=rb, tq=tq: e.tensor_tensor(out=tq, in0=rb, in1=pt(0), op=ALU.mult),
               [rb, pt(0)], [tq])
            OP(SE, lambda e, i=i, tq=tq: e.activation(out=pt_i32(3 + i * 5), in_=tq, func=AF.Identity),
               [tq], [pt(3 + i * 5)])
            OP(SE, lambda e, i=i, rb=rb: e.activation(out=rb, in_=pt_i32(3 + i * 5), func=AF.Identity),
               [pt(3 + i * 5)], [rb])

        def g0_stageB(k):
            i = k % 2
            rb, tq = rbt[k % 3], tfq[i]
            OP(VE, lambda e, rb=rb, tq=tq: e.tensor_tensor(out=tq, in0=tq, in1=rb, op=ALU.subtract),
               [tq, rb], [tq])
            OP(VE, lambda e, rb=rb, tq=tq: e.scalar_tensor_tensor(
                out=rb, in0=tq, scalar=0.5, in1=rb, op0=ALU.is_gt, op1=ALU.add),
               [tq, rb], [rb])
            OP(VE, lambda e, i=i, rb=rb, tq=tq: e.scalar_tensor_tensor(
                out=qv[i], in0=tq, scalar=-0.5, in1=rb, op0=ALU.is_lt, op1=ALU.subtract),
               [tq, rb], [qvr[i]])
            DMA(h2q.ap()[k * 128:(k + 1) * 128, :], qv[i], via="gpsimd")
            if k + 3 < NHC:
                DMA(rbt[k % 3], h2T.ap()[(k + 3) * 128:(k + 4) * 128, :], via="gpsimd")
            if k + 2 < NHC:
                quantize_w7(0, k + 2)
            f2_bundle(0, k, qv[i], qvr[i])

        g0_stageA(0)
        for k in range(NHC):
            if k + 1 < NHC:
                g0_stageA(k + 1)
            g0_stageB(k)

        def epilogue(g):
            for m in range(8):
                rchunk = rT[:, m * 2048 + g * 512: m * 2048 + g * 512 + 512]
                OP(VE, lambda e, m=m: e.tensor_scalar(
                    out=pool[:, 1024:1536], in0=slots[m], scalar1=epi2c[:, m:m + 1], scalar2=None, op0=ALU.mult),
                   [slots[m], epi2c[:]], [pool[:, 1024:1536]])
                OP(VE, lambda e, rchunk=rchunk: e.scalar_tensor_tensor(
                    out=rchunk, in0=rchunk, scalar=2.0, in1=pool[:, 1024:1536],
                    op0=ALU.mult, op1=ALU.add),
                   [rchunk, pool[:, 1024:1536]], [rchunk])

        epilogue(0)

        for g in range(1, 4):
            for kk in range(4):
                DMA(h2qb[kk], h2q.ap()[kk * 128:(kk + 1) * 128, :], via="gpsimd")
                quantize_w7(g, kk)
            for k in range(NHC):
                f2_bundle(g, k, h2qb[k % 4], h2qbr[k % 4])
                if k + 4 < NHC:
                    DMA(h2qb[k % 4], h2q.ap()[(k + 4) * 128:(k + 5) * 128, :], via="gpsimd")
                    quantize_w7(g, k + 4)
            epilogue(g)
        for m in range(8):
            chunk = rT[:, m * 2048:(m + 1) * 2048]
            OP(SE, lambda e, chunk=chunk, m=m: e.activation(out=sqbuf, in_=chunk, func=AF.Square,
                                                            accum_out=misc[:, 16 + m:17 + m]),
               [chunk], [sqbuf, misc[:, 16 + m:17 + m]])
            OP(VE, lambda e, chunk=chunk, m=m: e.tensor_reduce(
                out=misc[:, 8 + m:9 + m], in_=chunk, axis=AX.X, op=ALU.add),
               [chunk], [misc[:, 8 + m:9 + m]])
        OP(VE, lambda e: e.tensor_scalar(out=misc[:, 8:16], in0=misc[:, 8:16], scalar1=1.0 / D, scalar2=None, op0=ALU.mult),
           [misc[:, 8:16]], [misc[:, 8:16]])
        OP(VE, lambda e: e.tensor_scalar(out=misc[:, 16:24], in0=misc[:, 16:24], scalar1=1.0 / D, scalar2=None, op0=ALU.mult),
           [misc[:, 16:24]], [misc[:, 16:24]])
        OP(VE, lambda e: e.tensor_tensor(out=misc[:, 24:32], in0=misc[:, 8:16], in1=misc[:, 8:16], op=ALU.mult),
           [misc[:, 8:16]], [misc[:, 24:32]])
        OP(VE, lambda e: e.tensor_tensor(out=misc[:, 16:24], in0=misc[:, 16:24], in1=misc[:, 24:32], op=ALU.subtract),
           [misc[:, 16:24], misc[:, 24:32]], [misc[:, 16:24]])
        OP(VE, lambda e: e.tensor_scalar(out=misc[:, 16:24], in0=misc[:, 16:24], scalar1=EPS, scalar2=None, op0=ALU.add),
           [misc[:, 16:24]], [misc[:, 16:24]])
        OP(SE, lambda e: e.activation(out=misc[:, 16:24], in_=misc[:, 16:24], func=AF.Sqrt),
           [misc[:, 16:24]], [misc[:, 16:24]])
        OP(VE, lambda e: e.reciprocal(out=misc[:, 16:24], in_=misc[:, 16:24]), [misc[:, 16:24]], [misc[:, 16:24]])
        OP(VE, lambda e: e.tensor_tensor(out=misc[:, 24:32], in0=misc[:, 8:16], in1=misc[:, 16:24], op=ALU.mult),
           [misc[:, 8:16], misc[:, 16:24]], [misc[:, 24:32]])
        OP(VE, lambda e: e.tensor_scalar(out=misc[:, 24:32], in0=misc[:, 24:32], scalar1=-1.0, scalar2=None, op0=ALU.mult),
           [misc[:, 24:32]], [misc[:, 24:32]])
        for m in range(8):
            chunk = rT[:, m * 2048:(m + 1) * 2048]
            ob = outbufs[m % 2]
            OP(SE, lambda e, chunk=chunk, m=m, ob=ob: e.activation(out=ob, in_=chunk, func=AF.Identity,
                                                                  scale=misc[:, 16 + m:17 + m], bias=misc[:, 24 + m:25 + m]),
               [chunk, misc[:, 16 + m:17 + m], misc[:, 24 + m:25 + m]], [ob])
            DMA(out[m * 128:(m + 1) * 128, :], ob, via="gpsimd")

        finals = dict(S.eng_n)
        last_waited = {e: {} for e in ("tensor", "vector", "scalar", "gpsimd", "sync")}
        plan = []
        for (eng, kind, key, idx, fn, deps) in S.steps:
            w = {}
            for (dk, di) in deps:
                need = (di + 1) * (16 if (dk[0:2] in ("sy", "ge") and dk not in ("gpsimd",)) else 1)
                if need > w.get(dk, 0):
                    w[dk] = need
            lw = last_waited[eng]
            ws = []
            for dk, need in sorted(w.items()):
                if need > lw.get(dk, 0):
                    lw[dk] = need
                    ws.append((dk, need))
            plan.append((eng, kind, key, fn, ws))

        with nc.Block() as block:
            def runner(eng_name):
                def _(e):
                    for (my_eng, kind, key, fn, ws) in plan:
                        if my_eng != eng_name:
                            continue
                        for (dk, need) in ws:
                            e.wait_ge(sem[dk], need)
                        inst = fn(e)
                        if isinstance(inst, list):
                            inst = inst[-1]
                        inst.then_inc(sem[key], 16 if kind == "d" else 1)
                    if eng_name == "gpsimd":
                        for k, n in finals.items():
                            v = n * (16 if (k[0:2] in ("sy", "ge") and k != "gpsimd") else 1)
                            if v > 0:
                                e.wait_ge(sem[k], v)
                return _
            block.tensor(runner("tensor"))
            block.vector(runner("vector"))
            block.scalar(runner("scalar"))
            block.gpsimd(runner("gpsimd"))
            block.sync(runner("sync"))
    return nc


def _layout_inputs(L0_attn_X, L0_attn_W_in, L0_attn_b_in, L0_attn_W_out, L0_attn_b_out,
                   L0_attn_gamma, L0_attn_beta, L0_ffn_norm_w, L0_ffn_w_gv, L0_ffn_w_out):
    f32 = np.float32
    bf16 = ml_dtypes.bfloat16
    X = np.asarray(L0_attn_X, f32)
    W_in = np.ascontiguousarray(np.asarray(L0_attn_W_in, f32))
    W_out = np.ascontiguousarray(np.asarray(L0_attn_W_out, f32))
    w_gv = np.ascontiguousarray(np.asarray(L0_ffn_w_gv, f32))
    w_out2 = np.ascontiguousarray(np.asarray(L0_ffn_w_out, f32))

    wi = np.ascontiguousarray(W_in.reshape(16, 128, 48, 128).transpose(2, 1, 0, 3).reshape(48, 128, 2048))
    wo = np.ascontiguousarray(W_out.reshape(16, 128, 16, 128).transpose(2, 1, 0, 3).reshape(16, 128, 2048))
    wgv = np.ascontiguousarray(w_gv.reshape(128, 128, 16, 128).transpose(0, 3, 2, 1).reshape(128, 128, 2048))
    wo2 = np.ascontiguousarray(w_out2.reshape(4, 512, 64, 128).transpose(0, 2, 3, 1).reshape(4, 64, 128, 512))

    vecs = np.zeros((128, 112), f32)
    vecs[:, 0:48] = np.asarray(L0_attn_b_in, f32).reshape(48, 128).T
    vecs[:, 48:64] = np.asarray(L0_attn_b_out, f32).reshape(16, 128).T
    vecs[:, 64:80] = np.asarray(L0_attn_gamma, f32).reshape(16, 128).T
    vecs[:, 80:96] = np.asarray(L0_attn_beta, f32).reshape(16, 128).T
    vecs[:, 96:112] = np.asarray(L0_ffn_norm_w, f32).reshape(16, 128).T

    id128 = np.eye(128, dtype=f32)
    ones16 = np.ones((128, 128), bf16)
    onesrow = np.ones((1, 128), f32)
    onescol = np.ones((128, 1), f32)
    permM = np.zeros((128, 128), f32)
    for i in range(64):
        permM[2 * i, 2 * i + 1] = 1.0
        permM[2 * i + 1, 2 * i] = -1.0
    permM = permM.astype(bf16)

    inv_freq = 1.0 / (10000.0 ** (np.arange(0, 1024, 2, dtype=np.float64) / 1024.0))

    in_maps = []
    for c in range(8):
        b, h = c // 2, c % 2
        Xc = X[b, h * 1024:(h + 1) * 1024, :]
        xTc = np.ascontiguousarray(Xc.T).reshape(16, 128, 1024).transpose(1, 0, 2).reshape(128, 16384)
        pos = np.arange(h * 1024, (h + 1) * 1024, dtype=np.float64)
        f_idx = np.arange(1024) // 2
        ang = pos[None, :] * inv_freq[f_idx][:, None]
        ct = np.cos(ang).astype(np.float32).reshape(8, 128, 1024).astype(bf16)
        stt = np.sin(ang).astype(np.float32).reshape(8, 128, 1024).astype(bf16)
        shg = w_gv.reshape(-1)[c * 4194304:(c + 1) * 4194304].reshape(16, 128, 2048)
        shw = w_out2.reshape(-1)[c * 2097152:(c + 1) * 2097152].reshape(8, 128, 2048)
        in_maps.append(dict(
            xT=np.ascontiguousarray(xTc), wi=wi, wo=wo, wgv=wgv, wo2=wo2,
            ctab=np.ascontiguousarray(ct), stab=np.ascontiguousarray(stt),
            vecs=vecs, id128=id128, ones16=ones16, onesrow=onesrow, onescol=onescol,
            perm=permM, shg=np.ascontiguousarray(shg), shw=np.ascontiguousarray(shw)))
    return in_maps


def kernel(L0_attn_X, L0_attn_W_in, L0_attn_b_in, L0_attn_W_out, L0_attn_b_out,
           L0_attn_gamma, L0_attn_beta, L0_ffn_norm_w, L0_ffn_w_gv, L0_ffn_w_out):
    in_maps = _layout_inputs(L0_attn_X, L0_attn_W_in, L0_attn_b_in, L0_attn_W_out,
                             L0_attn_b_out, L0_attn_gamma, L0_attn_beta,
                             L0_ffn_norm_w, L0_ffn_w_gv, L0_ffn_w_out)
    nc = build()
    res = run_bass_kernel_spmd(nc, in_maps, core_ids=list(range(8)))
    y = np.empty((4, 2048, 2048), np.float32)
    for c in range(8):
        b, h = c // 2, c % 2
        y[b, h * 1024:(h + 1) * 1024, :] = res.results[c]["out"]
    return y


if __name__ == "__main__":
    build()
    print("build ok")


# version 15 (final)
# speedup vs baseline: 1.0330x; 1.0330x over previous
import os
import contextlib
import numpy as np
import ml_dtypes
import concourse.bass as bass
import concourse.mybir as mybir
from concourse.bass_utils import run_bass_kernel_spmd

F32 = mybir.dt.float32
BF16 = mybir.dt.bfloat16
I32 = mybir.dt.int32
AF = mybir.ActivationFunctionType
ALU = mybir.AluOpType
AX = mybir.AxisListType

P = 128
D = 2048
NCH = 16
TOK = 1024
HID = 8192
NHC = 64
EPS = 1e-5
QB = 127.0
N_GV = float(16384 * 2048)
N_WOUT = float(2048 * 8192)
DEBUG = os.environ.get("MK_DEBUG", "0") == "1"


def _region(a):
    ds = mybir.dt.size(a.dtype)
    span = 1
    for stp, cnt in a.ap[1:]:
        span += stp * (cnt - 1)
    lo = a.offset * ds
    return (a.tensor.name, lo, lo + span * ds)


class Sched:

    SY_LANES = 4
    GE_LANES = 2

    def __init__(self):
        self.steps = []
        keys = ["tensor", "vector", "scalar", "gpsimd"]
        keys += [f"sy{j}" for j in range(self.SY_LANES)]
        keys += [f"ge{j}" for j in range(self.GE_LANES)]
        self.keys = keys
        self.eng_n = {e: 0 for e in keys}
        self.lane_rr = {"sync": 0, "gpsimd": 0}
        self.writers = {}
        self.readers = {}

    def _dep_scan(self, reads, writes):
        deps = set()
        for (nm, lo, hi) in reads:
            for (l2, h2, stp) in self.writers.get(nm, ()):
                if l2 < hi and lo < h2:
                    deps.add(stp)
        for (nm, lo, hi) in writes:
            for (l2, h2, stp) in self.writers.get(nm, ()):
                if l2 < hi and lo < h2:
                    deps.add(stp)
            for (l2, h2, stp) in self.readers.get(nm, ()):
                if l2 < hi and lo < h2:
                    deps.add(stp)
        return deps

    def _record(self, reads, writes, stamp):
        for (nm, lo, hi) in writes:
            wl = self.writers.setdefault(nm, [])
            wl[:] = [(l2, h2, stp) for (l2, h2, stp) in wl if not (lo <= l2 and h2 <= hi)]
            wl.append((lo, hi, stamp))
            rl = self.readers.setdefault(nm, [])
            rl[:] = [(l2, h2, stp) for (l2, h2, stp) in rl if not (l2 < hi and lo < h2)]
        for (nm, lo, hi) in reads:
            self.readers.setdefault(nm, []).append((lo, hi, stamp))

    def add(self, eng, kind, fn, r, w):
        reads = [_region(a) for a in r if a is not None]
        writes = [_region(a) for a in w if a is not None]
        deps = self._dep_scan(reads, writes)
        if kind == "c":
            key = eng
        elif eng == "sync":
            key = f"sy{self.lane_rr['sync'] % self.SY_LANES}"
            self.lane_rr["sync"] += 1
        else:
            key = f"ge{self.lane_rr['gpsimd'] % self.GE_LANES}"
            self.lane_rr["gpsimd"] += 1
        idx = self.eng_n[key]
        self.eng_n[key] += 1
        if kind == "d" and idx > 0:
            deps = set(deps)
            deps.add((key, idx - 1))
        stamp = (key, idx)
        self._record(reads, writes, stamp)
        self.steps.append((eng, kind, key, idx, fn, deps))


def build():
    nc = bass.Bass("TRN2", target_bir_lowering=False)

    def din(name, sh, dt):
        return nc.dram_tensor(name, sh, dt, kind="ExternalInput").ap()

    xT = din("xT", [P, NCH * TOK], F32)
    wi = din("wi", [48, P, 2048], F32)
    wo = din("wo", [16, P, 2048], F32)
    wgv = din("wgv", [128, P, 2048], F32)
    wo2 = din("wo2", [4, 64, P, 512], F32)
    ctab = din("ctab", [8, P, TOK], BF16)
    stab = din("stab", [8, P, TOK], BF16)
    vecs = din("vecs", [P, 112], F32)
    id128 = din("id128", [P, P], F32)
    ones16 = din("ones16", [P, P], BF16)
    onesrow = din("onesrow", [1, P], F32)
    onescol = din("onescol", [P, 1], F32)
    perm = din("perm", [P, P], BF16)
    shg = din("shg", [16, P, 2048], F32)
    shw = din("shw", [8, P, 2048], F32)
    out = nc.dram_tensor("out", [TOK, D], F32, kind="ExternalOutput").ap()
    dbg = {}
    if DEBUG:
        for nm, sh, dt in [("dbg_a", [P, 16], F32), ("dbg_r", [P, NCH * TOK], F32),
                           ("dbg_t", [P, NCH * TOK], F32), ("dbg_hq", [P, NCH * TOK], BF16),
                           ("dbg_k1", [P, TOK], F32), ("dbg_mx2", [P, 8], F32),
                           ("dbg_sc", [P, 16], F32)]:
            dbg[nm] = nc.dram_tensor(nm, sh, dt, kind="ExternalOutput").ap()

    ccm_in = nc.dram_tensor("ccm_in", [1, 16], F32, kind="Internal")
    ccm_out = nc.dram_tensor("ccm_out", [1, 16], F32, kind="Internal")
    ccA_in = nc.dram_tensor("ccA_in", [P, 16], F32, kind="Internal")
    ccA_out = nc.dram_tensor("ccA_out", [P, 16], F32, kind="Internal")
    h2T = nc.dram_tensor("h2T", [HID, TOK], F32, kind="Internal")
    h2q = nc.dram_tensor("h2q", [HID, TOK], BF16, kind="Internal")

    st = contextlib.ExitStack()
    sb = lambda name, sh, dt: st.enter_context(nc.sbuf_tensor(name, sh, dt))
    with st:
        Xf = sb("Xf", [P, NCH * TOK], F32)
        arA = sb("arA", [P, NCH * TOK], F32)
        arB = sb("arB", [P, 8192], F32)
        pool = sb("pool", [P, 10 * 1024], F32)
        vec_s = sb("vec_s", [P, 112], F32)
        id_s = sb("id_s", [P, P], F32)
        on16 = sb("on16", [P, P], BF16)
        onr = sb("onr", [1, P], F32)
        onc = sb("onc", [P, 1], F32)
        pm_s = sb("pm_s", [P, P], BF16)
        ccrow = sb("ccrow", [1, 16], F32)
        acol = sb("acol", [P, 16], F32)
        a2col = sb("a2col", [P, 16], F32)
        mcol = sb("mcol", [P, 8], F32)
        mcol2 = sb("mcol2", [P, 8], F32)
        epi2c = sb("epi2c", [P, 8], F32)
        misc = sb("misc", [P, 32], F32)

        p2a = st.enter_context(nc.psum_tensor("p2a", [P, 1024], F32))
        p2b = st.enter_context(nc.psum_tensor("p2b", [P, 1024], F32))
        p2c = st.enter_context(nc.psum_tensor("p2c", [P, 1024], F32))
        p2d = st.enter_context(nc.psum_tensor("p2d", [P, 1024], F32))
        sem_keys = ["tensor", "vector", "scalar", "gpsimd"] + \
            [f"sy{j}" for j in range(Sched.SY_LANES)] + [f"ge{j}" for j in range(Sched.GE_LANES)]
        sem = {k: st.enter_context(nc.semaphore(k)) for k in sem_keys}

        def pt(i):
            return pool[:, i * 1024:(i + 1) * 1024]

        def pt_bf(i):
            return pool.ap().bitcast(BF16)[:, i * 2048:i * 2048 + 1024]

        def pt_bf2(i):
            return pool.ap().bitcast(BF16)[:, i * 2048 + 1024:i * 2048 + 2048]

        def pt_lo(i):
            return pool[:, i * 1024:i * 1024 + 512]

        def pt_hi(i):
            return pool[:, i * 1024 + 512:(i + 1) * 1024]

        def pt_i32(i):
            return pool.ap().bitcast(I32)[:, i * 1024:(i + 1) * 1024]

        arA_bf = arA.ap().bitcast(BF16)
        arB_bf = arB.ap().bitcast(BF16)
        Xf_bf = Xf.ap().bitcast(BF16)

        S = Sched()

        def OP(eng, fn, r, w):
            S.add(eng, "c", fn, r, w)

        def DMA(out_ap, in_ap, via="sync"):
            S.add(via, "d", lambda en, o=out_ap, i=in_ap: en.dma_start(out=o, in_=i),
                  [in_ap], [out_ap])

        TE, VE, SE, GE = "tensor", "vector", "scalar", "gpsimd"

        DMA(vec_s[:], vecs)
        DMA(id_s[:], id128)
        DMA(on16[:], ones16)
        DMA(onr[:], onesrow)
        DMA(onc[:], onescol)
        DMA(pm_s[:], perm)
        OP(VE, lambda e: e.memset(ccrow[:], 0.0), [], [ccrow[:]])
        OP(VE, lambda e: e.memset(misc[:, 28:29], -1.0), [], [misc[:, 28:29]])
        neg1 = misc[:, 28:29]
        OP(VE, lambda e: e.memset(misc[:, 30:31], -0.5), [], [misc[:, 30:31]])
        negh = misc[:, 30:31]

        for q in range(4):
            DMA(Xf[:, q * 4096:(q + 1) * 4096], xT[:, q * 4096:(q + 1) * 4096])

        wst = [arA[:, 10240:12288], arA[:, 12288:14336]]
        w16v = [arA_bf[:, 28672:28672 + 2048], arA_bf[:, 30720:30720 + 2048]]
        w16r = [arA[:, 14336:15360], arA[:, 15360:16384]]

        def loadw(slot, src):
            DMA(wst[slot], src)
            OP(SE, lambda e, slot=slot: e.activation(out=w16v[slot], in_=wst[slot], func=AF.Identity),
               [wst[slot]], [w16r[slot]])

        x16s = [pt_bf(0), pt_bf(3)]
        x16r = [pt_lo(0), pt_lo(3)]
        xsqs = [pt_bf(1), pt_bf(4)]
        xsqr = [pt_lo(1), pt_lo(4)]
        for ch in range(NCH):
            j = ch % 2
            xch = Xf[:, ch * TOK:(ch + 1) * TOK]
            OP(SE, lambda e, xch=xch, j=j: e.activation(out=x16s[j], in_=xch, func=AF.Identity),
               [xch], [x16r[j]])
            OP(SE, lambda e, j=j: e.activation(out=xsqs[j], in_=x16s[j], func=AF.Square),
               [x16r[j]], [xsqr[j]])
            for hf in range(2):
                OP(TE, lambda e, ch=ch, hf=hf, j=j: e.matmul(
                    p2a[:, hf * 512:(hf + 1) * 512], on16[:], x16s[j][:, hf * 512:(hf + 1) * 512],
                    start=(ch == 0), stop=(ch == NCH - 1)),
                   [on16[:], x16r[j]], [p2a[:, hf * 512:(hf + 1) * 512]])
                OP(TE, lambda e, ch=ch, hf=hf, j=j: e.matmul(
                    p2b[:, hf * 512:(hf + 1) * 512], on16[:], xsqs[j][:, hf * 512:(hf + 1) * 512],
                    start=(ch == 0), stop=(ch == NCH - 1)),
                   [on16[:], xsqr[j]], [p2b[:, hf * 512:(hf + 1) * 512]])
        loadw(0, wi[16, :, :])
        loadw(1, wi[32, :, :])
        OP(VE, lambda e: e.tensor_scalar(out=pt(0), in0=p2a[:], scalar1=1.0 / D, scalar2=None, op0=ALU.mult),
           [p2a[:]], [pt(0)])
        OP(VE, lambda e: e.tensor_scalar(out=pt(1), in0=p2b[:], scalar1=1.0 / D, scalar2=None, op0=ALU.mult),
           [p2b[:]], [pt(1)])
        OP(VE, lambda e: e.tensor_tensor(out=pt(2), in0=pt(0), in1=pt(0), op=ALU.mult), [pt(0)], [pt(2)])
        OP(VE, lambda e: e.tensor_tensor(out=pt(1), in0=pt(1), in1=pt(2), op=ALU.subtract), [pt(1), pt(2)], [pt(1)])
        OP(VE, lambda e: e.tensor_scalar(out=pt(1), in0=pt(1), scalar1=EPS, scalar2=None, op0=ALU.add), [pt(1)], [pt(1)])
        OP(SE, lambda e: e.activation(out=pt(1), in_=pt(1), func=AF.Ln), [pt(1)], [pt(1)])
        OP(SE, lambda e: e.activation(out=pt(1), in_=pt(1), func=AF.Exp, scale=negh),
           [pt(1), negh], [pt(1)])
        Xn16 = arB_bf
        for ch in range(NCH):
            xch = Xf[:, ch * TOK:(ch + 1) * TOK]
            xn = arB_bf[:, ch * TOK:(ch + 1) * TOK]
            xnr = arB[:, ch * 512:(ch + 1) * 512]
            OP(VE, lambda e, xch=xch: e.tensor_tensor(out=pt(2), in0=xch, in1=pt(0), op=ALU.subtract),
               [xch, pt(0)], [pt(2)])
            OP(VE, lambda e: e.tensor_tensor(out=pt(2), in0=pt(2), in1=pt(1), op=ALU.mult),
               [pt(2), pt(1)], [pt(2)])
            OP(VE, lambda e, ch=ch, xn=xn, xnr=xnr: e.tensor_scalar(
                out=xn, in0=pt(2), scalar1=vec_s[:, 64 + ch:65 + ch], scalar2=vec_s[:, 80 + ch:81 + ch],
                op0=ALU.mult, op1=ALU.add), [pt(2), vec_s[:]], [xnr])

        ctb = [arA_bf[:, 0:1024], arA_bf[:, 1024:2048]]
        ctb_r = [arA[:, 0:512], arA[:, 512:1024]]
        stb = [arA_bf[:, 18432:19456], arA_bf[:, 19456:20480]]
        stb_r = [arA[:, 9216:9728], arA[:, 9728:10240]]
        z16 = arA_bf[:, 2048:2048 + NCH * TOK]
        z16_r = arA[:, 1024:1024 + 8192]

        def mm_bundle(w16, rhs_bf, slot, nk, k0=0, k1=None):
            if k1 is None:
                k1 = nk
            def fn(e, w16=w16, rhs_bf=rhs_bf, slot=slot, nk=nk, k0=k0, k1=k1):
                last = None
                for k in range(k0, k1):
                    for hf in range(2):
                        last = e.matmul(slot[:, hf * 512:(hf + 1) * 512],
                                        w16[:, k * 128:(k + 1) * 128],
                                        rhs_bf[:, k * TOK + hf * 512: k * TOK + hf * 512 + 512],
                                        start=(k == 0), stop=(k == nk - 1))
                return last
            return fn

        def bundle_split(w16, wreg, rhs_bf, slot):
            OP(TE, mm_bundle(w16, rhs_bf, slot, NCH, 0, 8), [wreg, arB[:, 0:4096]], [slot[:]])
            OP(TE, mm_bundle(w16, rhs_bf, slot, NCH, 8, 16), [wreg, arB[:, 4096:8192]], [slot[:]])

        def bundle_quarters(w16, wreg, rhs_bf, slot):
            for q in range(4):
                OP(TE, mm_bundle(w16, rhs_bf, slot, NCH, q * 4, (q + 1) * 4),
                   [wreg, arB[:, q * 2048:(q + 1) * 2048]], [slot[:]])

        def permmm(src16, srcr, slot):
            def emit():
                OP(TE, lambda e, src16=src16, slot=slot: [
                    e.matmul(slot[:, 0:512], pm_s[:], src16[:, 0:512], start=True, stop=True),
                    e.matmul(slot[:, 512:1024], pm_s[:], src16[:, 512:1024], start=True, stop=True)][-1],
                   [pm_s[:], srcr], [slot[:]])
            return emit

        k16s = [pt_bf(0), pt_bf(2)]
        k16r = [pt_lo(0), pt_lo(2)]
        sqKs = [pt_bf2(0), pt_bf2(2)]
        sqKr = [pt_hi(0), pt_hi(2)]
        v16s = [pt_bf(1), pt_bf(3)]
        v16r = [pt_lo(1), pt_lo(3)]
        sqVs = [pt_bf2(1), pt_bf2(3)]
        sqVr = [pt_hi(1), pt_hi(3)]
        rotKs = [pt_bf(4), pt_bf(5)]
        rotKr = [pt_lo(4), pt_lo(5)]
        rotVs = [pt_bf2(4), pt_bf2(5)]
        rotVr = [pt_hi(4), pt_hi(5)]

        def stats_mm(ch, src16, srcr, slot):
            for hf in range(2):
                OP(TE, lambda e, ch=ch, hf=hf, src16=src16, slot=slot: e.matmul(
                    slot[:, hf * 512:(hf + 1) * 512], on16[:], src16[:, hf * 512:(hf + 1) * 512],
                    start=(ch == 0), stop=(ch == NCH - 1)),
                   [on16[:], srcr], [slot[:, hf * 512:(hf + 1) * 512]])

        DMA(ctb[0], ctab[0, :, :])
        DMA(stb[0], stab[0, :, :])
        for ch in range(NCH):
            j = ch % 2
            mK, mV = 16 + ch, 32 + ch
            bundle_split(w16v[0], w16r[0], Xn16, p2a)
            OP(SE, lambda e, mK=mK, j=j: e.activation(
                out=k16s[j], in_=p2a[:], func=AF.Identity, bias=vec_s[:, mK:mK + 1]),
               [p2a[:], vec_s[:]], [k16r[j]])
            if ch < 8:
                permmm(k16s[j], k16r[j], p2a)()
                OP(SE, lambda e, j=j: e.activation(out=rotKs[j], in_=p2a[:], func=AF.Identity),
                   [p2a[:]], [rotKr[j]])
            if ch < NCH - 1:
                loadw(0, wi[16 + ch + 1, :, :])
            bundle_split(w16v[1], w16r[1], Xn16, p2b)
            if ch > 0:
                jp = (ch - 1) % 2
                stats_mm(ch - 1, sqKs[jp], sqKr[jp], p2c)
                stats_mm(ch - 1, sqVs[jp], sqVr[jp], p2d)
            OP(SE, lambda e, mV=mV, j=j: e.activation(
                out=v16s[j], in_=p2b[:], func=AF.Identity, bias=vec_s[:, mV:mV + 1]),
               [p2b[:], vec_s[:]], [v16r[j]])
            if ch < 8:
                permmm(v16s[j], v16r[j], p2b)()
                OP(SE, lambda e, j=j: e.activation(out=rotVs[j], in_=p2b[:], func=AF.Identity),
                   [p2b[:]], [rotVr[j]])
            if ch < NCH - 1:
                loadw(1, wi[32 + ch + 1, :, :])
            OP(SE, lambda e, j=j: e.activation(out=sqKs[j], in_=k16s[j], func=AF.Square),
               [k16r[j]], [sqKr[j]])
            OP(SE, lambda e, j=j: e.activation(out=sqVs[j], in_=v16s[j], func=AF.Square),
               [v16r[j]], [sqVr[j]])
            zch = z16[:, ch * TOK:(ch + 1) * TOK]
            zchr = z16_r[:, ch * 512:(ch + 1) * 512]
            if ch < 8:
                ti = ch % 2
                if ch < 7:
                    DMA(ctb[(ch + 1) % 2], ctab[ch + 1, :, :])
                    DMA(stb[(ch + 1) % 2], stab[ch + 1, :, :])
                OP(VE, lambda e, j=j, ti=ti: e.tensor_tensor(out=pt_bf(6), in0=k16s[j], in1=ctb[ti], op=ALU.mult),
                   [k16r[j], ctb_r[ti]], [pt_lo(6)])
                OP(VE, lambda e, j=j, ti=ti: e.tensor_tensor(out=pt_bf2(6), in0=rotKs[j], in1=stb[ti], op=ALU.mult),
                   [rotKr[j], stb_r[ti]], [pt_hi(6)])
                OP(VE, lambda e: e.tensor_tensor(out=pt_bf(7), in0=pt_bf(6), in1=pt_bf2(6), op=ALU.add),
                   [pt_lo(6), pt_hi(6)], [pt_lo(7)])
                OP(VE, lambda e, j=j, ti=ti: e.tensor_tensor(out=pt_bf(6), in0=v16s[j], in1=ctb[ti], op=ALU.mult),
                   [v16r[j], ctb_r[ti]], [pt_lo(6)])
                OP(VE, lambda e, j=j, ti=ti: e.tensor_tensor(out=pt_bf2(6), in0=rotVs[j], in1=stb[ti], op=ALU.mult),
                   [rotVr[j], stb_r[ti]], [pt_hi(6)])
                OP(VE, lambda e: e.tensor_tensor(out=pt_bf2(7), in0=pt_bf(6), in1=pt_bf2(6), op=ALU.add),
                   [pt_lo(6), pt_hi(6)], [pt_hi(7)])
                OP(VE, lambda e, zch=zch, zchr=zchr: e.tensor_tensor(
                    out=zch, in0=pt_bf(7), in1=pt_bf2(7), op=ALU.mult),
                   [pt_lo(7), pt_hi(7)], [zchr])
            else:
                OP(VE, lambda e, zch=zch, zchr=zchr, j=j: e.tensor_tensor(
                    out=zch, in0=k16s[j], in1=v16s[j], op=ALU.mult), [k16r[j], v16r[j]], [zchr])
        stats_mm(NCH - 1, sqKs[(NCH - 1) % 2], sqKr[(NCH - 1) % 2], p2c)
        stats_mm(NCH - 1, sqVs[(NCH - 1) % 2], sqVr[(NCH - 1) % 2], p2d)

        OP(VE, lambda e: e.tensor_copy(out=pt(7), in_=p2c[:]), [p2c[:]], [pt(7)])
        OP(VE, lambda e: e.tensor_tensor(out=pt(7), in0=pt(7), in1=p2d[:], op=ALU.mult), [pt(7), p2d[:]], [pt(7)])
        OP(SE, lambda e: e.activation(out=pt(7), in_=pt(7), func=AF.Ln), [pt(7)], [pt(7)])
        OP(SE, lambda e: e.activation(out=pt(7), in_=pt(7), func=AF.Exp, scale=negh),
           [pt(7), negh], [pt(7)])
        for ch in range(NCH):
            zch = z16[:, ch * TOK:(ch + 1) * TOK]
            zchr = z16_r[:, ch * 512:(ch + 1) * 512]
            OP(VE, lambda e, zch=zch: e.tensor_tensor(out=pt(2), in0=zch, in1=pt(7), op=ALU.mult),
               [zchr, pt(7)], [pt(2)])
            OP(VE, lambda e, ch=ch: e.tensor_reduce(
                out=acol[:, ch:ch + 1], in_=pt(2), axis=AX.X, op=ALU.add), [pt(2)], [acol[:, ch:ch + 1]])
        DMA(ccA_in.ap(), acol[:], via="gpsimd")
        OP(GE, lambda e: e.collective_compute(
            "AllReduce", ALU.add, replica_groups=[[0, 1], [2, 3], [4, 5], [6, 7]],
            ins=[ccA_in.ap().opt()], outs=[ccA_out.ap().opt()]),
           [ccA_in.ap()], [ccA_out.ap()])
        DMA(a2col[:], ccA_out.ap(), via="gpsimd")
        OP(VE, lambda e: e.tensor_scalar(out=a2col[:], in0=a2col[:],
                                         scalar1=0.5, scalar2=0.5, op0=ALU.mult, op1=ALU.add),
           [a2col[:]], [a2col[:]])
        if DEBUG:
            DMA(dbg["dbg_a"], a2col[:], via="gpsimd")

        AQ16 = arA_bf[:, 0:NCH * TOK]
        AQ16_r = arA[:, 0:8192]
        loadw(0, wi[0, :, :])
        for ch in range(NCH):
            j = ch % 2
            slot = p2a if j == 0 else p2b
            aq = AQ16[:, ch * TOK:(ch + 1) * TOK]
            aqr = arA[:, ch * 512:(ch + 1) * 512]
            OP(TE, mm_bundle(w16v[j], Xn16, slot, NCH), [w16r[j], arB[:]], [slot[:]])
            if ch < NCH - 1:
                loadw((ch + 1) % 2, wi[ch + 1, :, :])
            OP(SE, lambda e, ch=ch, aq=aq, slot=slot: e.activation(
                out=aq, in_=slot[:], func=AF.Identity, bias=vec_s[:, ch:ch + 1]),
               [slot[:], vec_s[:]], [aqr])
            OP(VE, lambda e, ch=ch, aq=aq, aqr=aqr: e.tensor_scalar(
                out=aq, in0=aq, scalar1=a2col[:, ch:ch + 1], scalar2=None, op0=ALU.mult),
               [aqr, a2col[:]], [aqr])

        shstg = [arB[:, 0:2048], arB[:, 2048:4096]]

        def shard_unit(i):
            src = shg[i, :, :] if i < 16 else shw[i - 16, :, :]
            stg = shstg[i % 2]
            red = misc[:, 8:10] if i % 2 == 0 else misc[:, 24:26]
            red1 = misc[:, 10:11] if i % 2 == 0 else misc[:, 26:27]
            DMA(stg, src, via="gpsimd")
            OP(VE, lambda e, stg=stg, red=red: e.tensor_reduce(
                out=red, in_=stg.rearrange("p (a b) -> p a b", a=2),
                axis=AX.X, op=ALU.add, apply_absolute_value=True), [stg], [red])
            OP(VE, lambda e, red=red, red1=red1: e.tensor_reduce(
                out=red1, in_=red, axis=AX.X, op=ALU.add), [red], [red1])
            if i == 0:
                OP(VE, lambda e, red1=red1: e.tensor_copy(out=misc[:, 11:12], in_=red1),
                   [red1], [misc[:, 11:12]])
            elif i == 16:
                OP(VE, lambda e, red1=red1: e.tensor_copy(out=misc[:, 12:13], in_=red1),
                   [red1], [misc[:, 12:13]])
            else:
                dst = misc[:, 11:12] if i < 16 else misc[:, 12:13]
                OP(VE, lambda e, dst=dst, red1=red1: e.tensor_tensor(
                    out=dst, in0=dst, in1=red1, op=ALU.add), [dst, red1], [dst])

        loadw(0, wo[0, :, :])
        for m in range(16):
            j = m % 2
            slot = p2a if j == 0 else p2b
            xch = Xf[:, m * TOK:(m + 1) * TOK]
            OP(TE, mm_bundle(w16v[j], AQ16, slot, NCH), [w16r[j], AQ16_r], [slot[:]])
            if m < 15:
                loadw((m + 1) % 2, wo[m + 1, :, :])
            OP(VE, lambda e, xch=xch, slot=slot: e.scalar_tensor_tensor(
                out=pt(2), in0=xch, scalar=2.0, in1=slot[:], op0=ALU.mult, op1=ALU.add),
               [xch, slot[:]], [pt(2)])
            OP(VE, lambda e, m=m, xch=xch: e.tensor_scalar(
                out=xch, in0=pt(2), scalar1=vec_s[:, 48 + m:49 + m], scalar2=None, op0=ALU.add),
               [pt(2), vec_s[:]], [xch])
            OP(SE, lambda e, xch=xch, j=j: e.activation(out=x16s[j], in_=xch, func=AF.Identity),
               [xch], [x16r[j]])
            OP(SE, lambda e, j=j: e.activation(out=xsqs[j], in_=x16s[j], func=AF.Square),
               [x16r[j]], [xsqr[j]])
            if m > 0:
                jp = (m - 1) % 2
                stats_mm(m - 1, x16s[jp], x16r[jp], p2c)
                stats_mm(m - 1, xsqs[jp], xsqr[jp], p2d)
            if m < 12:
                shard_unit(2 * m)
                shard_unit(2 * m + 1)
        stats_mm(15, x16s[15 % 2], x16r[15 % 2], p2c)
        stats_mm(15, xsqs[15 % 2], xsqr[15 % 2], p2d)

        OP(TE, lambda e: e.matmul(p2a[0:1, 0:1], onc[:], misc[:, 11:12], start=True, stop=True),
           [onc[:], misc[:, 11:12]], [p2a[0:1, 0:1]])
        OP(VE, lambda e: e.tensor_copy(out=ccrow[0:1, 0:1], in_=p2a[0:1, 0:1]),
           [p2a[0:1, 0:1]], [ccrow[0:1, 0:1]])
        OP(TE, lambda e: e.matmul(p2a[0:1, 1:2], onc[:], misc[:, 12:13], start=True, stop=True),
           [onc[:], misc[:, 12:13]], [p2a[0:1, 1:2]])
        OP(VE, lambda e: e.tensor_copy(out=ccrow[0:1, 1:2], in_=p2a[0:1, 1:2]),
           [p2a[0:1, 1:2]], [ccrow[0:1, 1:2]])
        DMA(ccm_in.ap(), ccrow[:], via="gpsimd")
        OP(GE, lambda e: e.collective_compute(
            "AllReduce", ALU.add, replica_groups=[list(range(8))],
            ins=[ccm_in.ap().opt()], outs=[ccm_out.ap().opt()]),
           [ccm_in.ap()], [ccm_out.ap()])
        DMA(ccrow[:], ccm_out.ap(), via="gpsimd")

        def emit_cc1_tail():
            OP(TE, lambda e: e.matmul(p2a[:, 0:16], onr[:], ccrow[:], start=True, stop=True),
               [onr[:], ccrow[:]], [p2a[:, 0:16]])
            OP(VE, lambda e: e.tensor_copy(out=a2col[:, 0:16], in_=p2a[:, 0:16]),
               [p2a[:, 0:16]], [a2col[:]])
            for (c0, c1, nrm) in ((0, 16, N_GV), (1, 17, N_WOUT)):
                OP(VE, lambda e, c0=c0, c1=c1, nrm=nrm: e.tensor_scalar(
                    out=misc[:, c1:c1 + 1], in0=a2col[:, c0:c0 + 1],
                    scalar1=1.0 / nrm, scalar2=1e-5, op0=ALU.mult, op1=ALU.max),
                   [a2col[:]], [misc[:, c1:c1 + 1]])
            OP(VE, lambda e: e.reciprocal(out=misc[:, 18:19], in_=misc[:, 16:17]), [misc[:, 16:17]], [misc[:, 18:19]])
            OP(VE, lambda e: e.reciprocal(out=misc[:, 19:20], in_=misc[:, 17:18]), [misc[:, 17:18]], [misc[:, 19:20]])
            for (dsti, srci, k) in ((20, 16, 0.5), (21, 16, -0.5), (22, 17, 0.5), (23, 17, -0.5)):
                OP(VE, lambda e, dsti=dsti, srci=srci, k=k: e.tensor_scalar(
                    out=misc[:, dsti:dsti + 1], in0=misc[:, srci:srci + 1], scalar1=k, scalar2=None, op0=ALU.mult),
                   [misc[:, srci:srci + 1]], [misc[:, dsti:dsti + 1]])
            if DEBUG:
                OP(VE, lambda e: e.tensor_copy(out=a2col[:, 2:3], in_=misc[:, 16:17]), [misc[:, 16:17]], [a2col[:, 2:3]])
                OP(VE, lambda e: e.tensor_copy(out=a2col[:, 3:4], in_=misc[:, 17:18]), [misc[:, 17:18]], [a2col[:, 3:4]])
                DMA(dbg["dbg_sc"], a2col[:, 0:16], via="gpsimd")

        OP(VE, lambda e: e.tensor_scalar(out=pt(0), in0=p2c[:], scalar1=1.0 / D, scalar2=None, op0=ALU.mult),
           [p2c[:]], [pt(0)])
        OP(VE, lambda e: e.tensor_scalar(out=pt(1), in0=p2d[:], scalar1=1.0 / D, scalar2=None, op0=ALU.mult),
           [p2d[:]], [pt(1)])
        OP(VE, lambda e: e.tensor_tensor(out=pt(2), in0=pt(0), in1=pt(0), op=ALU.mult), [pt(0)], [pt(2)])
        OP(VE, lambda e: e.tensor_tensor(out=pt(1), in0=pt(1), in1=pt(2), op=ALU.subtract), [pt(1), pt(2)], [pt(1)])
        OP(VE, lambda e: e.tensor_scalar(out=pt(1), in0=pt(1), scalar1=EPS, scalar2=None, op0=ALU.add), [pt(1)], [pt(1)])
        OP(SE, lambda e: e.activation(out=pt(1), in_=pt(1), func=AF.Ln), [pt(1)], [pt(1)])
        OP(SE, lambda e: e.activation(out=pt(1), in_=pt(1), func=AF.Exp, scale=negh),
           [pt(1), negh], [pt(1)])
        for ch in range(NCH):
            xch = Xf[:, ch * TOK:(ch + 1) * TOK]
            OP(VE, lambda e, xch=xch: e.tensor_tensor(out=pt(2), in0=xch, in1=pt(0), op=ALU.subtract),
               [xch, pt(0)], [pt(2)])
            OP(VE, lambda e, xch=xch: e.tensor_tensor(out=xch, in0=pt(2), in1=pt(1), op=ALU.mult),
               [pt(2), pt(1)], [xch])
        if DEBUG:
            DMA(dbg["dbg_r"], Xf[:], via="gpsimd")

        for ch in range(NCH):
            j = ch % 2
            xch = Xf[:, ch * TOK:(ch + 1) * TOK]
            tch = arA[:, ch * TOK:(ch + 1) * TOK]
            OP(SE, lambda e, xch=xch, j=j: e.activation(out=x16s[j], in_=xch, func=AF.Square),
               [xch], [x16r[j]])
            if ch > 0:
                jp = (ch - 1) % 2
                stats_mm(ch - 1, x16s[jp], x16r[jp], p2c)
            OP(VE, lambda e, xch=xch, tch=tch, ch=ch: e.tensor_scalar(
                out=tch, in0=xch, scalar1=vec_s[:, 96 + ch:97 + ch], scalar2=None, op0=ALU.mult),
               [xch, vec_s[:]], [tch])
        stats_mm(15, x16s[15 % 2], x16r[15 % 2], p2c)
        tt = arA
        if DEBUG:
            DMA(dbg["dbg_t"], arA[:], via="gpsimd")
        OP(VE, lambda e: e.tensor_scalar(out=pt(0), in0=p2c[:], scalar1=1.0 / D, scalar2=EPS, op0=ALU.mult, op1=ALU.add),
           [p2c[:]], [pt(0)])
        OP(SE, lambda e: e.activation(out=pt(0), in_=pt(0), func=AF.Ln), [pt(0)], [pt(0)])
        OP(SE, lambda e: e.activation(out=pt(0), in_=pt(0), func=AF.Exp, scale=negh),
           [pt(0), negh], [pt(0)])
        OP(SE, lambda e: e.activation(out=pt(1), in_=tt[:, 0:TOK], func=AF.Abs), [tt[:, 0:TOK]], [pt(1)])
        abst = [pt(2), pt(5)]
        for ch in range(1, NCH):
            j = ch % 2
            tch = tt[:, ch * TOK:(ch + 1) * TOK]
            OP(SE, lambda e, tch=tch, j=j: e.activation(out=abst[j], in_=tch, func=AF.Abs), [tch], [abst[j]])
            OP(VE, lambda e, j=j: e.tensor_tensor(out=pt(1), in0=pt(1), in1=abst[j], op=ALU.max),
               [pt(1), abst[j]], [pt(1)])
        for j in range(8):
            OP(TE, lambda e, j=j: e.transpose(p2d[:, 0:128], pt(1)[:, j * 128:(j + 1) * 128], id_s[:]),
               [pt(1), id_s[:]], [p2d[:, 0:128]])
            OP(VE, lambda e, j=j: e.tensor_reduce(
                out=mcol[:, j:j + 1], in_=p2d[:, 0:128], axis=AX.X, op=ALU.max),
               [p2d[:, 0:128]], [mcol[:, j:j + 1]])
        emit_cc1_tail()
        for j in range(8):
            OP(TE, lambda e, j=j: e.transpose(p2d[0:1, j * 128:(j + 1) * 128], mcol[:, j:j + 1], id_s[:]),
               [mcol[:], id_s[:]], [p2d[0:1, j * 128:(j + 1) * 128]])
        OP(VE, lambda e: e.tensor_copy(out=pt(2)[0:1, :], in_=p2d[0:1, :]), [p2d[:]], [pt(2)])
        OP(VE, lambda e: e.tensor_tensor(out=pt(2)[0:1, :], in0=pt(2)[0:1, :], in1=pt(0)[0:1, :], op=ALU.mult),
           [pt(2), pt(0)], [pt(2)])
        OP(VE, lambda e: e.tensor_scalar(out=pt(2)[0:1, :], in0=pt(2)[0:1, :], scalar1=1e-5, scalar2=None, op0=ALU.max),
           [pt(2)], [pt(2)])
        for hf in range(2):
            OP(TE, lambda e, hf=hf: e.matmul(
                p2c[:, hf * 512:(hf + 1) * 512], onr[:], pt(2)[0:1, hf * 512:(hf + 1) * 512],
                start=True, stop=True), [onr[:], pt(2)], [p2c[:, hf * 512:(hf + 1) * 512]])
        OP(SE, lambda e: e.activation(out=pt(3), in_=p2c[:], func=AF.Ln), [p2c[:]], [pt(3)])
        OP(SE, lambda e: e.activation(out=pt(3), in_=pt(3), func=AF.Exp, scale=neg1),
           [pt(3), neg1], [pt(3)])
        OP(VE, lambda e: e.scalar_tensor_tensor(
            out=pt(4), in0=pt(0), scalar=QB, in1=pt(3), op0=ALU.mult, op1=ALU.mult),
           [pt(0), pt(3)], [pt(4)])
        OP(VE, lambda e: e.tensor_scalar(
            out=pt(5), in0=p2c[:], scalar1=misc[:, 16:17], scalar2=1.0 / QB, op0=ALU.mult, op1=ALU.mult),
           [p2c[:], misc[:, 16:17]], [pt(5)])
        if DEBUG:
            DMA(dbg["dbg_k1"], pt(4), via="gpsimd")
        Vst = [arA[:, 0:2048], arA[:, 2048:4096]]
        Gst = [arA[:, 4096:6144], arA[:, 6144:8192]]
        tmpVA, tmpVAr = arA_bf[:, 16384:18432], arA[:, 8192:9216]
        tmpVB, tmpVBr = arA_bf[:, 18432:20480], arA[:, 9216:10240]
        tmpGA, tmpGAr = arA_bf[:, 20480:22528], arA[:, 10240:11264]
        tmpGB, tmpGBr = arA_bf[:, 22528:24576], arA[:, 11264:12288]
        TV = [arA_bf[:, 24576:26624], arA_bf[:, 26624:28672]]
        TVr = [arA[:, 12288:13312], arA[:, 13312:14336]]
        TG = [arA_bf[:, 28672:30720], arA_bf[:, 30720:32768]]
        TGr = [arA[:, 14336:15360], arA[:, 15360:16384]]

        def tern_quant(src, mk, tn, tnr, hi_col, nhi_col):
            OP(VE, lambda e, src=src, mk=mk: e.tensor_scalar(
                out=mk, in0=src, scalar1=misc[:, hi_col:hi_col + 1], scalar2=None, op0=ALU.is_gt),
               [src, misc[:, hi_col:hi_col + 1]], [mk])
            OP(VE, lambda e, src=src, mk=mk, tn=tn, tnr=tnr: e.scalar_tensor_tensor(
                out=tn, in0=src, scalar=misc[:, nhi_col:nhi_col + 1], in1=mk, op0=ALU.is_lt, op1=ALU.subtract),
               [src, mk, misc[:, nhi_col:nhi_col + 1]], [tnr])

        Mv = arA[:, 8192:10240]
        Mg = arA[:, 10240:12288]

        def q_val(a):
            i = a % 2
            DMA(Vst[i], wgv[a, :, :])
            tern_quant(Vst[i], Mv, TV[i], TVr[i], 20, 21)

        def q_gate(a):
            i = a % 2
            DMA(Gst[i], wgv[64 + a, :, :])
            tern_quant(Gst[i], Mg, TG[i], TGr[i], 20, 21)


        hq = arB_bf
        hv = [pt(6), pt(8)]
        hi32 = [7, 9]
        hbk = [pt(2), pt(0)]
        hs1 = [pt(3), pt(1)]

        def hq_mult(ch):
            j = ch % 2
            tch = tt[:, ch * TOK:(ch + 1) * TOK]
            OP(VE, lambda e, tch=tch, j=j: e.tensor_tensor(out=hv[j], in0=tch, in1=pt(4), op=ALU.mult),
               [tch, pt(4)], [hv[j]])

        def hq_casts(ch):
            j = ch % 2
            OP(SE, lambda e, j=j: e.activation(out=pt_i32(hi32[j]), in_=hv[j], func=AF.Identity),
               [hv[j]], [pt(hi32[j])])
            OP(SE, lambda e, j=j: e.activation(out=hbk[j], in_=pt_i32(hi32[j]), func=AF.Identity),
               [pt(hi32[j])], [hbk[j]])

        def hq_corr(ch):
            j = ch % 2
            hqc = arB_bf[:, ch * TOK:(ch + 1) * TOK]
            hqr = arB[:, ch * 512:(ch + 1) * 512]
            OP(VE, lambda e, j=j: e.tensor_tensor(out=hv[j], in0=hv[j], in1=hbk[j], op=ALU.subtract),
               [hv[j], hbk[j]], [hv[j]])
            OP(VE, lambda e, j=j: e.scalar_tensor_tensor(
                out=hs1[j], in0=hv[j], scalar=0.5, in1=hbk[j], op0=ALU.is_gt, op1=ALU.add),
               [hv[j], hbk[j]], [hs1[j]])
            OP(VE, lambda e, j=j, hqc=hqc, hqr=hqr: e.scalar_tensor_tensor(
                out=hqc, in0=hv[j], scalar=-0.5, in1=hs1[j], op0=ALU.is_lt, op1=ALU.subtract),
               [hv[j], hs1[j]], [hqr])

        hq_mult(0)
        for ch in range(NCH):
            hq_casts(ch)
            if ch < NCH - 1:
                hq_mult(ch + 1)
            if ch == 11:
                q_val(0)
            if ch == 12:
                q_val(1)
            if ch == 14:
                q_gate(0)
            if ch == 15:
                q_gate(1)
            hq_corr(ch)
        if DEBUG:
            DMA(dbg["dbg_hq"], arB_bf[:, 0:NCH * TOK], via="gpsimd")

        vtile = [pt(1), pt(2)]
        gtile = [pt(3), pt(8)]
        htile = [pt(9), pt(7)]
        mx2 = pt(6)

        def pv_of(a):
            return p2a if a % 2 == 0 else p2c

        def emit_ht(a):
            i = a % 2
            gt, ht = gtile[i], htile[i]
            pv = pv_of(a)
            OP(VE, lambda e, gt=gt, ht=ht, pv=pv: e.tensor_tensor(out=ht, in0=gt, in1=pv[:], op=ALU.mult),
               [gt, pv[:]], [ht])
            if a == 0:
                OP(SE, lambda e, ht=ht: e.activation(out=mx2, in_=ht, func=AF.Abs), [ht], [mx2])
            else:
                OP(SE, lambda e, ht=ht, i=i: e.activation(out=vtile[i], in_=ht, func=AF.Abs),
                   [ht], [vtile[i]])
            DMA(h2T.ap()[a * 128:(a + 1) * 128, :], ht, via="gpsimd")

        def emit_max(a):
            if a == 0:
                return
            i = a % 2
            OP(VE, lambda e, i=i: e.tensor_tensor(out=mx2, in0=mx2, in1=vtile[i], op=ALU.max),
               [mx2, vtile[i]], [mx2])

        for a in range(NHC):
            i = a % 2
            pv = pv_of(a)
            pg = p2b if i == 0 else p2d
            gt = gtile[i]
            bsel = bundle_quarters if a <= 1 else bundle_split
            bsel(TV[i], TVr[i], hq, pv)
            bsel(TG[i], TGr[i], hq, pg)
            if 2 <= a + 1 < NHC:
                q_val(a + 1)
                q_gate(a + 1)
            if a >= 1:
                emit_ht(a - 1)
            if a >= 2:
                emit_max(a - 2)
            OP(VE, lambda e, gt=gt, pg=pg: e.tensor_tensor(out=gt, in0=pg[:], in1=pt(5), op=ALU.mult),
               [pg[:], pt(5)], [gt])
            OP(SE, lambda e, gt=gt: e.activation(out=gt, in_=gt, func=AF.Silu), [gt], [gt])
        emit_ht(NHC - 1)
        emit_max(NHC - 2)
        emit_max(NHC - 1)

        OP(VE, lambda e: e.tensor_tensor(out=mx2, in0=mx2, in1=pt(5), op=ALU.mult),
           [mx2, pt(5)], [mx2])
        for j in range(8):
            OP(TE, lambda e, j=j: e.transpose(p2d[:, 0:128], mx2[:, j * 128:(j + 1) * 128], id_s[:]),
               [mx2, id_s[:]], [p2d[:, 0:128]])
            OP(VE, lambda e, j=j: e.tensor_reduce(
                out=mcol2[:, j:j + 1], in_=p2d[:, 0:128], axis=AX.X, op=ALU.max),
               [p2d[:, 0:128]], [mcol2[:, j:j + 1]])
        OP(VE, lambda e: e.tensor_scalar(out=mcol2[:], in0=mcol2[:], scalar1=1e-5, scalar2=None, op0=ALU.max),
           [mcol2[:]], [mcol2[:]])
        if DEBUG:
            DMA(dbg["dbg_mx2"], mcol2[:], via="gpsimd")
        OP(VE, lambda e: e.tensor_scalar(
            out=epi2c[:], in0=mcol2[:], scalar1=misc[:, 17:18], scalar2=1.0 / QB, op0=ALU.mult, op1=ALU.mult),
           [mcol2[:], misc[:, 17:18]], [epi2c[:]])
        OP(VE, lambda e: e.reciprocal(out=mcol2[:], in_=mcol2[:]), [mcol2[:]], [mcol2[:]])
        OP(VE, lambda e: e.tensor_scalar(out=mcol2[:], in0=mcol2[:], scalar1=QB, scalar2=None, op0=ALU.mult),
           [mcol2[:]], [mcol2[:]])
        for j in range(8):
            OP(TE, lambda e, j=j: e.transpose(p2d[0:1, j * 128:(j + 1) * 128], mcol2[:, j:j + 1], id_s[:]),
               [mcol2[:], id_s[:]], [p2d[0:1, j * 128:(j + 1) * 128]])
        OP(VE, lambda e: e.tensor_copy(out=pt(2)[0:1, :], in_=p2d[0:1, :]), [p2d[:]], [pt(2)])
        for hf in range(2):
            OP(TE, lambda e, hf=hf: e.matmul(
                p2c[:, hf * 512:(hf + 1) * 512], onr[:], pt(2)[0:1, hf * 512:(hf + 1) * 512],
                start=True, stop=True), [onr[:], pt(2)], [p2c[:, hf * 512:(hf + 1) * 512]])
        OP(VE, lambda e: e.tensor_copy(out=pt(0), in_=p2c[:]), [p2c[:]], [pt(0)])
        OP(VE, lambda e: e.tensor_tensor(out=pt(0), in0=pt(0), in1=pt(5), op=ALU.mult),
           [pt(0), pt(5)], [pt(0)])

        rT = arA
        for tc in range(8):
            pd = p2d if tc % 2 == 0 else p2c
            for half in range(2):
                for fo in range(8):
                    fc = half * 8 + fo
                    src = Xf[:, fc * TOK + tc * 128: fc * TOK + tc * 128 + 128]
                    OP(TE, lambda e, src=src, fo=fo, pd=pd: e.transpose(
                        pd[:, fo * 128:(fo + 1) * 128], src, id_s[:]),
                       [src, id_s[:]], [pd[:, fo * 128:(fo + 1) * 128]])
                dstv = arA[:, tc * 2048 + half * 1024: tc * 2048 + half * 1024 + 1024]
                OP(SE, lambda e, dstv=dstv, pd=pd: e.activation(out=dstv, in_=pd[:], func=AF.Identity),
                   [pd[:]], [dstv])

        ternW = Xf_bf[:, 0:NHC * 512]
        ternWr = Xf[:, 0:NHC * 256]
        w7st = [arB[:, j * 512:(j + 1) * 512] for j in range(4)]
        w7mks = [arB[:, 2048:2560], arB[:, 4608:5120]]
        h2qb = [arB_bf[:, 5120 + j * TOK:5120 + (j + 1) * TOK] for j in range(4)]
        h2qbr = [arB[:, 2560 + j * 512:2560 + (j + 1) * 512] for j in range(4)]
        sqbuf = Xf[:, 0:2048]
        outbufs = [Xf[:, 2048:4096], Xf[:, 4096:6144]]
        slots = [p2a[:, 0:512], p2a[:, 512:1024], p2b[:, 0:512], p2b[:, 512:1024],
                 p2c[:, 0:512], p2c[:, 512:1024], p2d[:, 0:512], p2d[:, 512:1024]]
        psum_all = [p2a[:], p2b[:], p2c[:], p2d[:]]
        rbt = [pt(1), pt(2), pt(4)]
        tfq = [pt(6), pt(7)]
        qv = [pt_bf(9), pt_bf(5)]
        qvr = [pt_lo(9), pt_lo(5)]

        def quantize_w7(g, k):
            j = k % 4
            i = k % 2
            DMA(w7st[j], wo2[g, k, :, :])
            twk = ternW[:, k * 512:(k + 1) * 512]
            twkr = ternWr[:, k * 256:(k + 1) * 256]
            tern_quant(w7st[j], w7mks[i], twk, twkr, 22, 23)

        def f2_bundle(g, k, lhs_bf, lhs_r):
            if k == 0:
                for m in range(8):
                    OP(TE, lambda e, k=k, lhs_bf=lhs_bf, m=m: e.matmul(
                        slots[m], lhs_bf[:, m * 128:(m + 1) * 128],
                        ternW[:, k * 512:(k + 1) * 512], start=True, stop=False),
                       [lhs_r, ternWr[:, k * 256:(k + 1) * 256]], [slots[m]])
                return
            def fn(e, k=k, lhs_bf=lhs_bf):
                last = None
                for m in range(8):
                    last = e.matmul(slots[m], lhs_bf[:, m * 128:(m + 1) * 128],
                                    ternW[:, k * 512:(k + 1) * 512],
                                    start=(k == 0), stop=(k == NHC - 1))
                return last
            OP(TE, fn, [lhs_r, ternWr[:, k * 256:(k + 1) * 256]], psum_all)

        for kk in range(3):
            DMA(rbt[kk], h2T.ap()[kk * 128:(kk + 1) * 128, :], via="gpsimd")
        quantize_w7(0, 0)
        quantize_w7(0, 1)

        def g0_stageA(k):
            i = k % 2
            rb, tq = rbt[k % 3], tfq[i]
            OP(VE, lambda e, rb=rb, tq=tq: e.tensor_tensor(out=tq, in0=rb, in1=pt(0), op=ALU.mult),
               [rb, pt(0)], [tq])
            OP(SE, lambda e, i=i, tq=tq: e.activation(out=pt_i32(3 + i * 5), in_=tq, func=AF.Identity),
               [tq], [pt(3 + i * 5)])
            OP(SE, lambda e, i=i, rb=rb: e.activation(out=rb, in_=pt_i32(3 + i * 5), func=AF.Identity),
               [pt(3 + i * 5)], [rb])

        def g0_stageB(k):
            i = k % 2
            rb, tq = rbt[k % 3], tfq[i]
            OP(VE, lambda e, rb=rb, tq=tq: e.tensor_tensor(out=tq, in0=tq, in1=rb, op=ALU.subtract),
               [tq, rb], [tq])
            OP(VE, lambda e, rb=rb, tq=tq: e.scalar_tensor_tensor(
                out=rb, in0=tq, scalar=0.5, in1=rb, op0=ALU.is_gt, op1=ALU.add),
               [tq, rb], [rb])
            OP(VE, lambda e, i=i, rb=rb, tq=tq: e.scalar_tensor_tensor(
                out=qv[i], in0=tq, scalar=-0.5, in1=rb, op0=ALU.is_lt, op1=ALU.subtract),
               [tq, rb], [qvr[i]])
            DMA(h2q.ap()[k * 128:(k + 1) * 128, :], qv[i], via="gpsimd")
            if k + 3 < NHC:
                DMA(rbt[k % 3], h2T.ap()[(k + 3) * 128:(k + 4) * 128, :], via="gpsimd")
            if k + 2 < NHC:
                quantize_w7(0, k + 2)
            f2_bundle(0, k, qv[i], qvr[i])

        g0_stageA(0)
        for k in range(NHC):
            if k + 1 < NHC:
                g0_stageA(k + 1)
            g0_stageB(k)

        def epilogue(g):
            for m in range(8):
                rchunk = rT[:, m * 2048 + g * 512: m * 2048 + g * 512 + 512]
                OP(VE, lambda e, m=m: e.tensor_scalar(
                    out=pool[:, 1024:1536], in0=slots[m], scalar1=epi2c[:, m:m + 1], scalar2=None, op0=ALU.mult),
                   [slots[m], epi2c[:]], [pool[:, 1024:1536]])
                OP(VE, lambda e, rchunk=rchunk: e.scalar_tensor_tensor(
                    out=rchunk, in0=rchunk, scalar=2.0, in1=pool[:, 1024:1536],
                    op0=ALU.mult, op1=ALU.add),
                   [rchunk, pool[:, 1024:1536]], [rchunk])

        epilogue(0)

        for g in range(1, 4):
            for kk in range(4):
                DMA(h2qb[kk], h2q.ap()[kk * 128:(kk + 1) * 128, :], via="gpsimd")
                quantize_w7(g, kk)
            for k in range(NHC):
                f2_bundle(g, k, h2qb[k % 4], h2qbr[k % 4])
                if k + 4 < NHC:
                    DMA(h2qb[k % 4], h2q.ap()[(k + 4) * 128:(k + 5) * 128, :], via="gpsimd")
                    quantize_w7(g, k + 4)
            epilogue(g)
        for m in range(8):
            chunk = rT[:, m * 2048:(m + 1) * 2048]
            OP(SE, lambda e, chunk=chunk, m=m: e.activation(out=sqbuf, in_=chunk, func=AF.Square,
                                                            accum_out=misc[:, 16 + m:17 + m]),
               [chunk], [sqbuf, misc[:, 16 + m:17 + m]])
            OP(VE, lambda e, chunk=chunk, m=m: e.tensor_reduce(
                out=misc[:, 8 + m:9 + m], in_=chunk, axis=AX.X, op=ALU.add),
               [chunk], [misc[:, 8 + m:9 + m]])
        OP(VE, lambda e: e.tensor_scalar(out=misc[:, 8:16], in0=misc[:, 8:16], scalar1=1.0 / D, scalar2=None, op0=ALU.mult),
           [misc[:, 8:16]], [misc[:, 8:16]])
        OP(VE, lambda e: e.tensor_scalar(out=misc[:, 16:24], in0=misc[:, 16:24], scalar1=1.0 / D, scalar2=None, op0=ALU.mult),
           [misc[:, 16:24]], [misc[:, 16:24]])
        OP(VE, lambda e: e.tensor_tensor(out=misc[:, 24:32], in0=misc[:, 8:16], in1=misc[:, 8:16], op=ALU.mult),
           [misc[:, 8:16]], [misc[:, 24:32]])
        OP(VE, lambda e: e.tensor_tensor(out=misc[:, 16:24], in0=misc[:, 16:24], in1=misc[:, 24:32], op=ALU.subtract),
           [misc[:, 16:24], misc[:, 24:32]], [misc[:, 16:24]])
        OP(VE, lambda e: e.tensor_scalar(out=misc[:, 16:24], in0=misc[:, 16:24], scalar1=EPS, scalar2=None, op0=ALU.add),
           [misc[:, 16:24]], [misc[:, 16:24]])
        OP(SE, lambda e: e.activation(out=misc[:, 16:24], in_=misc[:, 16:24], func=AF.Sqrt),
           [misc[:, 16:24]], [misc[:, 16:24]])
        OP(VE, lambda e: e.reciprocal(out=misc[:, 16:24], in_=misc[:, 16:24]), [misc[:, 16:24]], [misc[:, 16:24]])
        OP(VE, lambda e: e.tensor_tensor(out=misc[:, 24:32], in0=misc[:, 8:16], in1=misc[:, 16:24], op=ALU.mult),
           [misc[:, 8:16], misc[:, 16:24]], [misc[:, 24:32]])
        OP(VE, lambda e: e.tensor_scalar(out=misc[:, 24:32], in0=misc[:, 24:32], scalar1=-1.0, scalar2=None, op0=ALU.mult),
           [misc[:, 24:32]], [misc[:, 24:32]])
        for m in range(8):
            chunk = rT[:, m * 2048:(m + 1) * 2048]
            ob = outbufs[m % 2]
            OP(SE, lambda e, chunk=chunk, m=m, ob=ob: e.activation(out=ob, in_=chunk, func=AF.Identity,
                                                                  scale=misc[:, 16 + m:17 + m], bias=misc[:, 24 + m:25 + m]),
               [chunk, misc[:, 16 + m:17 + m], misc[:, 24 + m:25 + m]], [ob])
            DMA(out[m * 128:(m + 1) * 128, :], ob, via="gpsimd")

        finals = dict(S.eng_n)
        last_waited = {e: {} for e in ("tensor", "vector", "scalar", "gpsimd", "sync")}
        plan = []
        for (eng, kind, key, idx, fn, deps) in S.steps:
            w = {}
            for (dk, di) in deps:
                need = (di + 1) * (16 if (dk[0:2] in ("sy", "ge") and dk not in ("gpsimd",)) else 1)
                if need > w.get(dk, 0):
                    w[dk] = need
            lw = last_waited[eng]
            ws = []
            for dk, need in sorted(w.items()):
                if need > lw.get(dk, 0):
                    lw[dk] = need
                    ws.append((dk, need))
            plan.append((eng, kind, key, fn, ws))

        with nc.Block() as block:
            def runner(eng_name):
                def _(e):
                    for (my_eng, kind, key, fn, ws) in plan:
                        if my_eng != eng_name:
                            continue
                        for (dk, need) in ws:
                            e.wait_ge(sem[dk], need)
                        inst = fn(e)
                        if isinstance(inst, list):
                            inst = inst[-1]
                        inst.then_inc(sem[key], 16 if kind == "d" else 1)
                    if eng_name == "gpsimd":
                        for k, n in finals.items():
                            v = n * (16 if (k[0:2] in ("sy", "ge") and k != "gpsimd") else 1)
                            if v > 0:
                                e.wait_ge(sem[k], v)
                return _
            block.tensor(runner("tensor"))
            block.vector(runner("vector"))
            block.scalar(runner("scalar"))
            block.gpsimd(runner("gpsimd"))
            block.sync(runner("sync"))
    return nc


def _layout_inputs(L0_attn_X, L0_attn_W_in, L0_attn_b_in, L0_attn_W_out, L0_attn_b_out,
                   L0_attn_gamma, L0_attn_beta, L0_ffn_norm_w, L0_ffn_w_gv, L0_ffn_w_out):
    f32 = np.float32
    bf16 = ml_dtypes.bfloat16
    X = np.asarray(L0_attn_X, f32)
    W_in = np.ascontiguousarray(np.asarray(L0_attn_W_in, f32))
    W_out = np.ascontiguousarray(np.asarray(L0_attn_W_out, f32))
    w_gv = np.ascontiguousarray(np.asarray(L0_ffn_w_gv, f32))
    w_out2 = np.ascontiguousarray(np.asarray(L0_ffn_w_out, f32))

    wi = np.ascontiguousarray(W_in.reshape(16, 128, 48, 128).transpose(2, 1, 0, 3).reshape(48, 128, 2048))
    wo = np.ascontiguousarray(W_out.reshape(16, 128, 16, 128).transpose(2, 1, 0, 3).reshape(16, 128, 2048))
    wgv = np.ascontiguousarray(w_gv.reshape(128, 128, 16, 128).transpose(0, 3, 2, 1).reshape(128, 128, 2048))
    wo2 = np.ascontiguousarray(w_out2.reshape(4, 512, 64, 128).transpose(0, 2, 3, 1).reshape(4, 64, 128, 512))

    vecs = np.zeros((128, 112), f32)
    vecs[:, 0:48] = np.asarray(L0_attn_b_in, f32).reshape(48, 128).T
    vecs[:, 48:64] = np.asarray(L0_attn_b_out, f32).reshape(16, 128).T
    vecs[:, 64:80] = np.asarray(L0_attn_gamma, f32).reshape(16, 128).T
    vecs[:, 80:96] = np.asarray(L0_attn_beta, f32).reshape(16, 128).T
    vecs[:, 96:112] = np.asarray(L0_ffn_norm_w, f32).reshape(16, 128).T

    id128 = np.eye(128, dtype=f32)
    ones16 = np.ones((128, 128), bf16)
    onesrow = np.ones((1, 128), f32)
    onescol = np.ones((128, 1), f32)
    permM = np.zeros((128, 128), f32)
    for i in range(64):
        permM[2 * i, 2 * i + 1] = 1.0
        permM[2 * i + 1, 2 * i] = -1.0
    permM = permM.astype(bf16)

    inv_freq = 1.0 / (10000.0 ** (np.arange(0, 1024, 2, dtype=np.float64) / 1024.0))

    in_maps = []
    for c in range(8):
        b, h = c // 2, c % 2
        Xc = X[b, h * 1024:(h + 1) * 1024, :]
        xTc = np.ascontiguousarray(Xc.T).reshape(16, 128, 1024).transpose(1, 0, 2).reshape(128, 16384)
        pos = np.arange(h * 1024, (h + 1) * 1024, dtype=np.float64)
        f_idx = np.arange(1024) // 2
        ang = pos[None, :] * inv_freq[f_idx][:, None]
        ct = np.cos(ang).astype(np.float32).reshape(8, 128, 1024).astype(bf16)
        stt = np.sin(ang).astype(np.float32).reshape(8, 128, 1024).astype(bf16)
        shg = w_gv.reshape(-1)[c * 4194304:(c + 1) * 4194304].reshape(16, 128, 2048)
        shw = w_out2.reshape(-1)[c * 2097152:(c + 1) * 2097152].reshape(8, 128, 2048)
        in_maps.append(dict(
            xT=np.ascontiguousarray(xTc), wi=wi, wo=wo, wgv=wgv, wo2=wo2,
            ctab=np.ascontiguousarray(ct), stab=np.ascontiguousarray(stt),
            vecs=vecs, id128=id128, ones16=ones16, onesrow=onesrow, onescol=onescol,
            perm=permM, shg=np.ascontiguousarray(shg), shw=np.ascontiguousarray(shw)))
    return in_maps


def kernel(L0_attn_X, L0_attn_W_in, L0_attn_b_in, L0_attn_W_out, L0_attn_b_out,
           L0_attn_gamma, L0_attn_beta, L0_ffn_norm_w, L0_ffn_w_gv, L0_ffn_w_out):
    in_maps = _layout_inputs(L0_attn_X, L0_attn_W_in, L0_attn_b_in, L0_attn_W_out,
                             L0_attn_b_out, L0_attn_gamma, L0_attn_beta,
                             L0_ffn_norm_w, L0_ffn_w_gv, L0_ffn_w_out)
    nc = build()
    res = run_bass_kernel_spmd(nc, in_maps, core_ids=list(range(8)))
    y = np.empty((4, 2048, 2048), np.float32)
    for c in range(8):
        b, h = c // 2, c % 2
        y[b, h * 1024:(h + 1) * 1024, :] = res.results[c]["out"]
    return y


if __name__ == "__main__":
    build()
    print("build ok")
